# version 16
# baseline (speedup 1.0000x reference)
import math
import numpy as np
import ml_dtypes
from contextlib import ExitStack
import concourse.bass as bass
import concourse.mybir as mybir
from concourse.bass_utils import run_bass_kernel_spmd

F32 = mybir.dt.float32
BF16 = mybir.dt.bfloat16
I32 = mybir.dt.int32
AF = mybir.ActivationFunctionType
ALU = mybir.AluOpType
AXL = mybir.AxisListType

NEG = -1.0e30

CFG_FULL = dict(E=1024, H=16, HID=4096, NE=8, T=2048, NSEQ=2, R=8)
CFG_SIM = dict(E=128, H=2, HID=512, NE=8, T=256, NSEQ=2, R=8)

_CHAIN_OPS = {
    "activation", "copy", "add", "mul", "sqrt", "square", "sign", "memset",
    "memzero", "tensor_copy", "tensor_tensor", "tensor_add", "tensor_sub",
    "tensor_mul", "tensor_max", "tensor_scalar", "tensor_single_scalar",
    "tensor_scalar_add", "tensor_scalar_sub", "tensor_scalar_mul",
    "tensor_scalar_max", "tensor_scalar_min", "tensor_relu", "tensor_reduce",
    "reduce_max", "reduce_sum", "reciprocal", "select", "copy_predicated",
    "tensor_tensor_reduce", "scalar_tensor_tensor", "tensor_tensor_scan",
    "iota", "bn_stats", "bn_aggr", "max", "max_index", "stream_shuffle",
    "affine_select",
}


class Ctr:

    def __init__(self, h):
        self.h = h
        self.n = 0

    def inc(self, instr, k=1):
        instr.then_inc(self.h, k)
        self.n += k
        return self.n

    def wait(self, eng, n=None):
        v = self.n if n is None else n
        if v > 0:
            eng.wait_ge(self.h, v)


class Chain:

    def __init__(self, eng, ctr):
        self._e = eng
        self._c = ctr

    def __getattr__(self, name):
        e = self._e
        a = getattr(e, name)
        if name not in _CHAIN_OPS:
            return a
        c = self._c

        def wrapped(*args, **kw):
            if c.n > 0:
                e.wait_ge(c.h, c.n)
            instr = a(*args, **kw)
            c.inc(instr)
            return instr

        return wrapped


def build(cfg, debug=False):
    E, H, HID, NE, T, NSEQ, R = (cfg["E"], cfg["H"], cfg["HID"], cfg["NE"],
                                 cfg["T"], cfg["NSEQ"], cfg["R"])
    DH = 64
    assert E == H * DH
    EC = E // 128
    HC = HID // 128
    NT = T * NSEQ
    NTC = NT // 128
    TBL = min(512, NT)
    NTB = NT // TBL
    TCB = TBL // 128
    QTN = T // 128
    PB = min(512, T)
    NPB = T // PB
    assert NPB <= 4
    EW = min(512, E)
    EH = E // EW
    SCL = 0.125
    LORA_S = 2.0
    SW = min(2 * T, 2048)
    CAP = 768 if T >= 2048 else 256
    CAPC = CAP // 128
    NCAP = NE * CAP
    CSTEPS = int(math.log2(NT))
    assert 2 ** CSTEPS == NT
    start = 2 ** (-2 ** (-(math.log2(H) - 3)))
    slopes = [start * start ** i for i in range(H)]
    W2G = 4 if HC % 4 == 0 else 1

    nc = bass.Bass("TRN2", target_bir_lowering=False)

    x_d = nc.dram_tensor("x", [NT, E], F32, kind="ExternalInput")
    out_d = nc.dram_tensor("out", [NT, E], F32, kind="ExternalOutput")
    wqT_d = nc.dram_tensor("wqT", [E, E], F32, kind="ExternalInput")
    wkT_d = nc.dram_tensor("wkT", [E, E], F32, kind="ExternalInput")
    wvT_d = nc.dram_tensor("wvT", [E, E], F32, kind="ExternalInput")
    wpT_d = nc.dram_tensor("wpT", [E, E], F32, kind="ExternalInput")
    bqT_d = nc.dram_tensor("bqT", [E, R], F32, kind="ExternalInput")
    bkT_d = nc.dram_tensor("bkT", [E, R], F32, kind="ExternalInput")
    bvT_d = nc.dram_tensor("bvT", [E, R], F32, kind="ExternalInput")
    aqT_d = nc.dram_tensor("aqT", [R, E], F32, kind="ExternalInput")
    akT_d = nc.dram_tensor("akT", [R, E], F32, kind="ExternalInput")
    avT_d = nc.dram_tensor("avT", [R, E], F32, kind="ExternalInput")
    w1_d = nc.dram_tensor("w1", [NE * E, HID], F32, kind="ExternalInput")
    b1_d = nc.dram_tensor("b1", [NE, HID], F32, kind="ExternalInput")
    w2_d = nc.dram_tensor("w2", [NE * HID, E], F32, kind="ExternalInput")
    b2_d = nc.dram_tensor("b2", [NE, E], F32, kind="ExternalInput")
    wr_d = nc.dram_tensor("wr", [E, NE], F32, kind="ExternalInput")
    br_d = nc.dram_tensor("br", [NE, 1], F32, kind="ExternalInput")

    xs_d = nc.dram_tensor("xs", [E, NT], BF16, kind="Internal")
    xt_d = nc.dram_tensor("xt", [NT, E], BF16, kind="Internal")
    xg_d = nc.dram_tensor("xg", [NCAP, E], BF16, kind="Internal")
    zg_d = nc.dram_tensor("zg", [NCAP, E], BF16, kind="Internal")
    w1b_d = nc.dram_tensor("w1b", [NE * E, HID], BF16, kind="Internal")
    w2b_d = nc.dram_tensor("w2b", [NE * HID, E], BF16, kind="Internal")
    if debug:
        dxs_d = nc.dram_tensor("d_xs", [E, NT], BF16, kind="ExternalOutput")
        dz_d = nc.dram_tensor("d_z", [NT, E], BF16, kind="ExternalOutput")
        dm_d = nc.dram_tensor("d_m", [128, NE * NTC], F32, kind="ExternalOutput")
        dc_d = nc.dram_tensor("d_c", [128, E], F32, kind="ExternalOutput")
        dq_d = nc.dram_tensor("d_q", [128, EC * T], F32, kind="ExternalOutput")
        dk_d = nc.dram_tensor("d_k", [128, EC * T], F32, kind="ExternalOutput")
        dv_d = nc.dram_tensor("d_v", [128, (T // 128) * E], F32,
                              kind="ExternalOutput")
        do_d = nc.dram_tensor("d_o", [128, EC * T], F32, kind="ExternalOutput")

    es = ExitStack()

    def sb(name, shape, dt):
        return es.enter_context(nc.sbuf_tensor(name, shape, dt))

    ident_i = sb("ident_i", [128, 128], F32)
    ident_bf = sb("ident_bf", [128, 128], BF16)
    ident_f = sb("ident_f", [128, 128], F32)
    mtri = sb("mtri", [128, 128], F32)
    b1_fm = sb("b1_fm", [128, NE * HC], F32)
    rb1_bf = sb("rb1_bf", [128, NE * HC], BF16)
    bneg_fm = sb("bneg_fm", [128, NE * HC], F32)
    nrb_fm = sb("nrb_fm", [128, NE * HC], F32)
    rb1_fm = nrb_fm
    m_all = sb("m_all", [128, NE * NTC], F32) if debug else None
    b2_bf = sb("b2_bf", [8, E], BF16)
    wr_sb = sb("wr_sb", [128, EC * NE], F32)
    wr_hi = sb("wr_hi", [128, EC * NE], BF16)
    wr_lo = sb("wr_lo", [128, EC * NE], BF16)
    br_col = sb("br_col", [8, 1], F32)
    ones1 = sb("ones1", [1, 128], BF16)
    ones8 = sb("ones8", [8, 1], BF16)
    csb = sb("csb", [1, E], BF16)
    cbc = sb("cbc", [128, E], F32)
    lg_sb = sb("lg_sb", [8, 128], F32)
    lgt = sb("lgt", [128, 8], F32)
    lnc = sb("lncols", [128, 8], F32)
    dsti = sb("dsti", [128, NTC], I32)
    ones8f = sb("ones8f", [8, 1], F32)
    eoff = sb("eoff", [8, 1], F32)
    capc = sb("capc", [8, 1], F32)
    wes = ExitStack()
    w1sb = wes.enter_context(nc.sbuf_tensor("w1sb", [128, EC * HID], BF16))
    w2sb = wes.enter_context(nc.sbuf_tensor("w2sb", [128, HC * E], BF16))

    def sem(name):
        return Ctr(es.enter_context(nc.semaphore(name)))

    s_pre = sem("s_pre")
    g_pre = sem("g_pre")
    s_xin = [sem("s_xin0"), sem("s_xin1")]
    s_xsw = [sem("s_xsw0"), sem("s_xsw1")]
    s_xg = [sem("s_xg0"), sem("s_xg1")]
    g_w1 = sem("g_w1")
    g_w2 = sem("g_w2")
    g_cw = sem("g_cw")
    a_w1 = sem("a_w1")
    a_w2 = sem("a_w2")
    a_c = [sem("a_c0"), sem("a_c1")]
    a_z0 = sem("a_z0")
    s_zg = [sem("s_zg0"), sem("s_zg1")]
    s_xt = sem("s_xt")
    g_sc = [sem("g_sc0"), sem("g_sc1")]
    g_zi = [sem("g_zi0"), sem("g_zi1")]
    g_wf = sem("g_wf")
    g_ab = sem("g_ab")
    g_bb = sem("g_bb")
    s_xn3 = sem("s_xn3")
    s_zi = [sem("s_zi0"), sem("s_zi1")]
    s_xe = [sem("s_xe0"), sem("s_xe1")]
    s_out = [sem("s_out0"), sem("s_out1")]
    pe = sem("pe")
    ve = sem("ve")
    ae = sem("ae")
    ge = sem("ge")

    block = es.enter_context(nc.Block())
    p1es = ExitStack()
    b1_sb = p1es.enter_context(nc.sbuf_tensor("b1_sb", [8, HID], F32))

    with es:
        def sb1(name, shape, dt):
            return p1es.enter_context(nc.sbuf_tensor(name, shape, dt))

        xbuf = [sb1("xbuf0", [128, E], F32), sb1("xbuf1", [128, E], F32)]
        xnf = [sb1("xnf0", [128, E], F32), sb1("xnf1", [128, E], F32)]
        xnh = sb1("xnh", [128, E], BF16)
        xnl = sb1("xnl", [128, E], BF16)
        stg = [sb1("stg0", [128, E], BF16), sb1("stg1", [128, E], BF16)]
        stl = sb1("stl", [128, E], BF16)
        mmt = sb1("mmt", [128, 8], F32)
        m8a = sb1("m8a", [8, NT], F32)
        assert HID >= NT
        c2b = b1_sb
        dsr = sb1("dsr", [1, 512], F32)

        @block.sync
        def _(sy):
            sy.dma_start(b1_sb[:, :], b1_d[:, :]).then_inc(s_pre.h, 16)
            sy.dma_start(br_col[:, :], br_d[:, :]).then_inc(s_pre.h, 16)
            sy.dma_start(wr_sb[:, :],
                         bass.AP(wr_d, 0, [[NE, 128], [128 * NE, EC], [1, NE]])
                         ).then_inc(s_pre.h, 16)
            s_pre.n = 48

        @block.gpsimd
        def _(g):
            cg = Chain(g, ge)
            cg.memset(stg[0][:, :], 0.0)
            cg.iota(ident_i[:, :], pattern=[[-1, 128]], base=0,
                    channel_multiplier=1, allow_small_or_imprecise_dtypes=True)
            cg.memset(ones1[:, :], 1.0)
            cg.memset(ones8[:, :], 1.0)
            cg.memset(ones8f[:, :], 1.0)
            cg.iota(eoff[:, :], pattern=[[0, 1]], base=-1, channel_multiplier=CAP,
                    allow_small_or_imprecise_dtypes=True)
            cg.iota(capc[:, :], pattern=[[0, 1]], base=CAP - 1,
                    channel_multiplier=CAP,
                    allow_small_or_imprecise_dtypes=True)
            g.dma_start(b2_bf[:, :], b2_d[:, :]).then_inc(g_pre.h, 16)
            g_pre.n += 16
            for e in range(NE):
                if e > 0:
                    g.wait_ge(g_cw.h, g_cw.n)
                g.dma_start(w1b_d[e * E:(e + 1) * E, :],
                            w1_d[e * E:(e + 1) * E, :]).then_inc(g_cw.h, 16)
                g_cw.n += 16
                g.wait_ge(g_cw.h, g_cw.n)
                g.dma_start(w2b_d[e * HID:(e + 1) * HID, :],
                            w2_d[e * HID:(e + 1) * HID, :]).then_inc(g_cw.h, 16)
                g_cw.n += 16

        @block.vector
        def _(v):
            cv = Chain(v, ve)
            ge.wait(v)
            cv.tensor_scalar(ident_bf[:, :], ident_i[:, :], 0.0, None,
                             ALU.is_equal)
            cv.tensor_scalar(ident_f[:, :], ident_i[:, :], 0.0, None,
                             ALU.is_equal)
            cv.tensor_scalar(mtri[:, :], ident_i[:, :], 0.0, NEG, ALU.is_lt,
                             ALU.mult)
            s_pre.wait(v)
            cv.tensor_copy(wr_hi[:, :], wr_sb[:, :])
            cv.tensor_tensor(wr_lo[:, :], wr_sb[:, :], wr_hi[:, :], ALU.subtract)

        @block.scalar
        def _(s):
            ge.wait(s, 1)
            for r0 in range(NCAP // 128):
                s.dma_start(xg_d[r0 * 128:(r0 + 1) * 128, :],
                            stg[0][:, :]).then_inc(a_z0.h, 16)
            a_z0.n += 16 * (NCAP // 128)
            s.wait_ge(g_cw.h, 32)
            for d in range(EC):
                s.dma_start(w1sb[:, d * HID:(d + 1) * HID],
                            w1b_d[d * 128:(d + 1) * 128, :]).then_inc(a_w1.h, 16)
            a_w1.n += 16 * EC
            for h0 in range(0, HC, W2G):
                s.dma_start(
                    bass.AP(w2sb, h0 * E, [[HC * E, 128], [E, W2G], [1, E]]),
                    bass.AP(w2b_d, h0 * 128 * E,
                            [[E, 128], [128 * E, W2G], [1, E]]),
                ).then_inc(a_w2.h, 16)
            a_w2.n += 16 * (HC // W2G)

        with nc.psum_tensor("ptr8", [128, 8], F32) as ptr8:
            for c in range(HC):
                @block.tensor
                def _(t, c=c):
                    ve.wait(t)
                    pe.inc(t.transpose(ptr8[:, :],
                                       b1_sb[0:8, c * 128:(c + 1) * 128],
                                       ident_f[0:8, 0:8]))

                @block.vector
                def _(v, c=c):
                    cv = Chain(v, ve)
                    pe.wait(v)
                    cv.tensor_copy(
                        bass.AP(b1_fm, c, [[NE * HC, 128], [HC, 8]]),
                        ptr8[:, :])

            @block.vector
            def _(v):
                cv = Chain(v, ve)
                cv.tensor_scalar(nrb_fm[:, :], b1_fm[:, :], 0.0, None, ALU.max)
                cv.tensor_copy(rb1_bf[:, :], nrb_fm[:, :])
                cv.tensor_scalar(bneg_fm[:, :], b1_fm[:, :], 0.0, None, ALU.min)
                cv.tensor_scalar(nrb_fm[:, :], nrb_fm[:, :], -1.0, None,
                                 ALU.mult)


        p1_sqrt = {}
        p1_hilo = {}
        p1_mmt_d = {}
        p1b_dst = [None]
        p1b_cur = [None]
        p1b_ve = {}
        p1_tr = {}
        p1_lg = {}
        p1_stage_d = {}
        IVE = 1.0 / E
        with nc.psum_tensor("pst0", [128, 128], BF16) as pst0, \
             nc.psum_tensor("pst1", [128, 128], BF16) as pst1, \
             nc.psum_tensor("lgps", [8, 128], F32) as lgps, \
             nc.psum_tensor("lgtp", [128, 8], F32) as lgtp, \
             nc.psum_tensor("m8ps", [8, 128], F32) as m8ps, \
             nc.psum_tensor("dstps", [1, 512], F32) as dstps, \
             nc.psum_tensor("dcps", [128, 1], F32) as dcps:
            p1_m8 = {}
            pst = [pst0, pst1]
            for j in range(min(2, NTC)):
                @block.sync
                def _(sy, j=j):
                    sy.dma_start(xbuf[j % 2][:, :],
                                 x_d[j * 128:(j + 1) * 128, :]
                                 ).then_inc(s_xin[j % 2].h, 16)
                    s_xin[j % 2].n += 16

            for j in range(NTC):
                bi = j % 2

                @block.vector
                def _(v, j=j, bi=bi):
                    cv = Chain(v, ve)
                    s_xin[bi].wait(v)
                    if j >= 1:
                        ae.wait(v, p1_sqrt[j - 1])
                    xt = xbuf[bi]
                    cv.scalar_tensor_tensor(xnf[bi][:, :], xt[:, :], 1.0,
                                            xt[:, :], ALU.mult, ALU.mult,
                                            accum_out=lnc[:, 1:2])
                    cv.reduce_sum(lnc[:, 0:1], xt[:, :], axis=AXL.X)
                    cv.tensor_scalar(lnc[:, 2:3], lnc[:, 0:1], IVE, None,
                                     ALU.mult)
                    cv.tensor_scalar(lnc[:, 3:4], lnc[:, 1:2], IVE, None,
                                     ALU.mult)
                    cv.scalar_tensor_tensor(lnc[:, 4:5], lnc[:, 2:3],
                                            lnc[:, 2:3], lnc[:, 3:4],
                                            ALU.mult, ALU.subtract)
                    cv.tensor_scalar(lnc[:, 5:6], lnc[:, 4:5], -1.0, 1e-5,
                                     ALU.mult, ALU.add)

                @block.scalar
                def _(s, j=j):
                    ca = Chain(s, ae)
                    ve.wait(s)
                    ca.activation(lnc[:, 6:7], lnc[:, 5:6], AF.Sqrt)
                    p1_sqrt[j] = ae.n

                @block.vector
                def _(v, j=j, bi=bi):
                    cv = Chain(v, ve)
                    ae.wait(v, p1_sqrt[j])
                    if j >= 1:
                        pe.wait(v, p1_tr[j - 1])
                        s_xt.wait(v)
                    cv.reciprocal(lnc[:, 7:8], lnc[:, 6:7])
                    cv.scalar_tensor_tensor(lnc[:, 4:5], lnc[:, 2:3], -1.0,
                                            lnc[:, 7:8], ALU.mult, ALU.mult)
                    cv.tensor_scalar(xnf[bi][:, :], xbuf[bi][:, :],
                                     lnc[:, 7:8], lnc[:, 4:5], ALU.mult,
                                     ALU.add)
                    cv.tensor_copy(xnh[:, :], xnf[bi][:, :])
                    cv.tensor_tensor(xnl[:, :], xnf[bi][:, :], xnh[:, :],
                                     ALU.subtract)
                    p1_hilo[j] = ve.n

                @block.sync
                def _(sy, j=j):
                    sy.wait_ge(ve.h, p1_hilo[j])
                    sy.dma_start(xt_d[j * 128:(j + 1) * 128, :],
                                 xnh[:, :]).then_inc(s_xt.h, 16)
                    s_xt.n += 16
                    if j + 2 < NTC:
                        sy.dma_start(xbuf[j % 2][:, :],
                                     x_d[(j + 2) * 128:(j + 3) * 128, :]
                                     ).then_inc(s_xin[j % 2].h, 16)
                        s_xin[j % 2].n += 16

                @block.tensor
                def _(t, j=j):
                    ve.wait(t, p1_hilo[j])
                    if j >= 2:
                        s_xsw[j % 2].wait(t)
                    ab = ae.n
                    for c in range(EC):
                        w = ab + c - 1
                        if w >= 1:
                            t.wait_ge(ae.h, w)
                        pe.inc(t.transpose(pst[c % 2][:, :],
                                           xnh[:, c * 128:(c + 1) * 128],
                                           ident_bf[:, :]))
                    for c in range(EC):
                        if c < 2:
                            c2 = EC - 1 if (EC - 1) % 2 == c % 2 else EC - 2
                            w = ab + c2 + 1 if c2 >= 0 else ab
                        else:
                            w = ab + EC + c - 1
                        if w >= 1:
                            t.wait_ge(ae.h, w)
                        pe.inc(t.transpose(pst[c % 2][:, :],
                                           xnl[:, c * 128:(c + 1) * 128],
                                           ident_bf[:, :]))
                    p1_tr[j] = pe.n

                @block.scalar
                def _(s, j=j):
                    ca = Chain(s, ae)
                    if j == 0:
                        a_z0.wait(s)
                    base = pe.n - 2 * EC
                    for c in range(EC):
                        s.wait_ge(pe.h, base + c + 1)
                        ca.activation(stg[j % 2][:, c * 128:(c + 1) * 128],
                                      pst[c % 2][:, :], AF.Copy)
                    for c in range(EC):
                        s.wait_ge(pe.h, base + EC + c + 1)
                        ca.activation(stl[:, c * 128:(c + 1) * 128],
                                      pst[c % 2][:, :], AF.Copy)
                    p1_stage_d[j] = ae.n

                @block.tensor
                def _(t, j=j):
                    ae.wait(t, p1_stage_d[j])
                    if j >= 1:
                        ve.wait(t, p1_lg[j - 1])
                    first = True
                    for c in range(EC):
                        t.matmul(lgps[:, :], wr_hi[:, c * NE:(c + 1) * NE],
                                 stg[j % 2][:, c * 128:(c + 1) * 128],
                                 start=first, stop=False)
                        first = False
                        t.matmul(lgps[:, :], wr_lo[:, c * NE:(c + 1) * NE],
                                 stg[j % 2][:, c * 128:(c + 1) * 128],
                                 start=False, stop=False)
                        mm = t.matmul(lgps[:, :], wr_hi[:, c * NE:(c + 1) * NE],
                                      stl[:, c * 128:(c + 1) * 128],
                                      start=False, stop=(c == EC - 1))
                    pe.inc(mm)

                @block.vector
                def _(v, j=j):
                    cv = Chain(v, ve)
                    pe.wait(v)
                    cv.tensor_scalar(lg_sb[:, :], lgps[:, :], br_col[:, 0:1],
                                     None, ALU.add)
                    p1_lg[j] = ve.n

                @block.tensor
                def _(t, j=j):
                    ve.wait(t, p1_lg[j])
                    pe.inc(t.transpose(lgtp[:, :], lg_sb[:, :],
                                       ident_f[0:8, 0:8]))

                @block.vector
                def _(v, j=j):
                    cv = Chain(v, ve)
                    pe.wait(v)
                    cv.tensor_copy(lgt[:, :], lgtp[:, :])
                    cv.reduce_max(lnc[:, 0:1], lgt[:, :], axis=AXL.X)
                    cv.tensor_scalar(mmt[:, :], lgt[:, :], lnc[:, 0:1], None,
                                     ALU.is_ge)
                    p1_mmt = ve.n
                    if debug:
                        cv.tensor_scalar(
                            bass.AP(m_all, j, [[NE * NTC, 128], [NTC, NE]]),
                            lgt[:, :], lnc[:, 0:1], None, ALU.is_ge)
                    p1_mmt_d[j] = p1_mmt

                @block.tensor
                def _(t, j=j):
                    ve.wait(t, p1_mmt_d[j])
                    if j >= 1:
                        ve.wait(t, p1_m8[j - 1])
                    pe.inc(t.transpose(m8ps[:, :], mmt[:, :],
                                       ident_f[:, :]))

                @block.vector
                def _(v, j=j):
                    cv = Chain(v, ve)
                    pe.wait(v)
                    cv.tensor_copy(m8a[0:8, j * 128:(j + 1) * 128], m8ps[:, :])
                    p1_m8[j] = ve.n

                @block.sync
                def _(sy, j=j):
                    sy.wait_ge(ae.h, p1_stage_d[j])
                    sy.dma_start(
                        bass.AP(xs_d, j * 128,
                                [[NT, 128], [128 * NT, EC], [1, 128]]),
                        bass.AP(stg[j % 2], 0, [[E, 128], [128, EC], [1, 128]]),
                    ).then_inc(s_xsw[j % 2].h, 16)
                    s_xsw[j % 2].n += 16


            @block.vector
            def _(v):
                cv = Chain(v, ve)
                ve.wait(v, p1_m8[NTC - 1])
                cur, oth = m8a, c2b
                step = 1
                while step < NT:
                    cv.tensor_copy(oth[0:8, 0:step], cur[0:8, 0:step])
                    cv.tensor_tensor(oth[0:8, step:NT], cur[0:8, step:NT],
                                     cur[0:8, 0:NT - step], ALU.add)
                    cur, oth = oth, cur
                    step *= 2
                cv.tensor_copy(oth[0:8, 0:1], cur[0:8, 0:1])
                cv.tensor_tensor(oth[0:8, 1:NT], cur[0:8, 1:NT],
                                 cur[0:8, 0:NT - 1], ALU.subtract)
                cv.scalar_tensor_tensor(cur[0:8, :], cur[0:8, :],
                                        eoff[:, 0:1], oth[0:8, :],
                                        ALU.add, ALU.mult)
                cv.tensor_scalar(cur[0:8, :], cur[0:8, :], capc[:, 0:1],
                                 None, ALU.min)
                p1b_dst[0] = ve.n
                p1b_cur[0] = cur

            for nb in range(NT // 512):
                @block.tensor
                def _(t, nb=nb):
                    ve.wait(t, p1b_dst[0] if nb == 0 else p1b_ve[nb - 1])
                    pe.inc(t.matmul(dstps[0:1, :], ones8f[:, 0:1],
                                    p1b_cur[0][0:8, nb * 512:(nb + 1) * 512],
                                    start=True, stop=True))

                @block.vector
                def _(v, nb=nb):
                    cv = Chain(v, ve)
                    pe.wait(v)
                    cv.tensor_scalar(dsr[0:1, :], dstps[0:1, :],
                                     float(NCAP - 1), None, ALU.min)

                for k in range(4):
                    @block.tensor
                    def _(t, nb=nb, k=k):
                        ve.wait(t)
                        pe.inc(t.transpose(dcps[:, :],
                                           dsr[0:1, k * 128:(k + 1) * 128],
                                           ident_f[0:1, 0:1]))

                    @block.vector
                    def _(v, nb=nb, k=k):
                        cv = Chain(v, ve)
                        pe.wait(v)
                        cv.tensor_copy(dsti[:, nb * 4 + k:nb * 4 + k + 1],
                                       dcps[:, :])
                        p1b_ve[nb] = ve.n

            for tc in range(NTC):
                @block.sync
                def _(sy, tc=tc):
                    if tc < 2:
                        s_xsw[0].wait(sy)
                        s_xsw[1].wait(sy)
                        pe.wait(sy)
                        sy.wait_ge(s_xt.h, 16 * NTC)
                    else:
                        g_sc[tc % 2].wait(sy)
                    sy.dma_start(stg[tc % 2][:, :],
                                 xt_d[tc * 128:(tc + 1) * 128, :]
                                 ).then_inc(s_xin[tc % 2].h, 16)
                    s_xin[tc % 2].n += 16

                @block.gpsimd
                def _(g, tc=tc):
                    s_xin[tc % 2].wait(g)
                    if tc == 0:
                        a_z0.wait(g)
                        g.wait_ge(ve.h, p1b_ve[NT // 512 - 1])
                    g.indirect_dma_start(
                        xg_d[:, :],
                        bass.IndirectOffsetOnAxis(dsti[:, tc:tc + 1], 0),
                        stg[tc % 2][:, :], None,
                    ).then_inc(g_sc[tc % 2].h, 16)
                    g_sc[tc % 2].n += 16


        p1es.close()
        p2es = ExitStack()

        def sb2(name, shape, dt):
            return p2es.enter_context(nc.sbuf_tensor(name, shape, dt))

        xtm = [sb2("xtm0", [128, E], BF16), sb2("xtm1", [128, E], BF16)]
        xgfm = sb2("xgfm", [128, EC * CAP], BF16)
        h1p = sb2("h1p", [128, HC * CAP], BF16)
        assert EC * CAP >= 2 * E
        zsc = [xgfm[:, 0:E], xgfm[:, E:2 * E]]

        p2_peA = {}
        p2_peB = {}
        p2_veh1 = {}
        p2_trcc = {}
        p2_trv = {}
        AB = [(0, min(512, CAP))] + ([(512, CAP)] if CAP > 512 else [])

        with nc.psum_tensor("trp0", [128, 128], BF16) as trp0, \
             nc.psum_tensor("trp1", [128, 128], BF16) as trp1, \
             nc.psum_tensor("h1ps0", [128, CAP], F32) as h1ps0, \
             nc.psum_tensor("h1ps1", [128, CAP], F32) as h1ps1, \
             nc.psum_tensor("zpsr", [128, E], F32) as zpsr:
            trp = [trp0, trp1]
            h1ps = [h1ps0, h1ps1]

            def emit_load(gcc):
                e2, cc2 = divmod(gcc, CAPC)
                if e2 >= NE:
                    return
                @block.sync
                def _(sy):
                    if gcc < 2:
                        g_sc[0].wait(sy)
                        g_sc[1].wait(sy)
                    else:
                        sy.wait_ge(pe.h, p2_trcc[gcc - 2])
                    sy.dma_start(
                        xtm[gcc % 2][:, :],
                        xg_d[e2 * CAP + cc2 * 128: e2 * CAP + (cc2 + 1) * 128, :],
                    ).then_inc(s_xg[gcc % 2].h, 16)
                    s_xg[gcc % 2].n += 16

            emit_load(0)
            emit_load(1)

            for e in range(NE):
                for cc in range(CAPC):
                    gcc = e * CAPC + cc

                    @block.tensor
                    def _(t, e=e, cc=cc, gcc=gcc):
                        s_xg[gcc % 2].wait(t)
                        if gcc == 0:
                            ve.wait(t)
                        vb = ve.n
                        for c in range(EC):
                            w = vb if c < 2 else vb + c - 1
                            if w >= 1:
                                t.wait_ge(ve.h, w)
                            pe.inc(t.transpose(trp[c % 2][:, :],
                                               xtm[gcc % 2][:, c * 128:(c + 1) * 128],
                                               ident_bf[:, :]))
                        p2_trcc[gcc] = pe.n

                    @block.vector
                    def _(v, e=e, cc=cc, gcc=gcc):
                        cv = Chain(v, ve)
                        pb = pe.n - EC
                        if cc == 0 and e > 0:
                            v.wait_ge(pe.h, p2_peA[e - 1])
                            s_zg[0].wait(v)
                            s_zg[1].wait(v)
                        for c in range(EC):
                            v.wait_ge(pe.h, pb + c + 1)
                            cv.tensor_copy(
                                xgfm[:, c * CAP + cc * 128: c * CAP + (cc + 1) * 128],
                                trp[c % 2][:, :])
                        p2_trv[e] = ve.n

                    emit_load(gcc + 2)

                @block.tensor
                def _(t, e=e):
                    ve.wait(t, p2_trv[e])
                    a_w1.wait(t, 16 * EC * (e + 1))
                    vb = ve.n
                    for h in range(HC):
                        w = vb + h - 1
                        if w >= 1:
                            t.wait_ge(ve.h, w)
                        for d in range(EC):
                            for (n0, n1) in AB:
                                mm = t.matmul(
                                    h1ps[h % 2][:, n0:n1],
                                    w1sb[:, d * HID + h * 128: d * HID + (h + 1) * 128],
                                    xgfm[:, d * CAP + n0: d * CAP + n1],
                                    start=(d == 0), stop=(d == EC - 1))
                        pe.inc(mm)
                    p2_peA[e] = pe.n

                @block.vector
                def _(v, e=e):
                    cv = Chain(v, ve)
                    pb = p2_peA[e] - HC
                    if e > 0:
                        v.wait_ge(pe.h, p2_peB[(e - 1, CAPC - 1)])
                    for h in range(HC):
                        v.wait_ge(pe.h, pb + h + 1)
                        cv.tensor_scalar(
                            h1p[:, h * CAP:(h + 1) * CAP],
                            h1ps[h % 2][:, 0:CAP],
                            bneg_fm[:, e * HC + h:e * HC + h + 1],
                            nrb_fm[:, e * HC + h:e * HC + h + 1],
                            ALU.add, ALU.max)
                    p2_veh1[e] = ve.n

                zvb = [0]

                @block.tensor
                def _(t, e=e):
                    ve.wait(t, p2_veh1[e])
                    a_w2.wait(t, 16 * (HC // W2G) * (e + 1))
                    zvb[0] = ve.n
                    for tc in range(CAPC):
                        if tc >= 1:
                            t.wait_ge(ve.h, zvb[0] + tc)
                        for h in range(HC):
                            for half in range(EH):
                                mm = t.matmul(
                                    zpsr[:, half * EW:(half + 1) * EW],
                                    h1p[:, h * CAP + tc * 128: h * CAP + tc * 128 + 128],
                                    w2sb[:, h * E + half * EW: h * E + (half + 1) * EW],
                                    start=(h == 0), stop=(h == HC - 1))
                        pe.inc(mm)
                        p2_peB[(e, tc)] = pe.n

                z_cp = {}

                @block.vector
                def _(v, e=e):
                    cv = Chain(v, ve)
                    for tc in range(CAPC):
                        gtc = e * CAPC + tc
                        v.wait_ge(pe.h, p2_peB[(e, tc)])
                        if gtc >= 2:
                            v.wait_ge(s_zg[gtc % 2].h, 16 * (gtc // 2))
                        cv.tensor_copy(zsc[tc % 2], zpsr[:, :])
                        z_cp[tc] = ve.n

                @block.sync
                def _(sy, e=e):
                    for tc in range(CAPC):
                        gtc = e * CAPC + tc
                        sy.wait_ge(ve.h, z_cp[tc])
                        sy.dma_start(
                            zg_d[e * CAP + tc * 128: e * CAP + (tc + 1) * 128, :],
                            zsc[tc % 2]).then_inc(s_zg[gtc % 2].h, 16)
                        s_zg[gtc % 2].n += 16

                @block.scalar
                def _(s, e=e):
                    if e + 1 < NE:
                        s.wait_ge(g_cw.h, 32 * (e + 2))
                        s.wait_ge(pe.h, p2_peA[e])
                        for d in range(EC):
                            s.dma_start(
                                w1sb[:, d * HID:(d + 1) * HID],
                                w1b_d[(e + 1) * E + d * 128:
                                      (e + 1) * E + (d + 1) * 128, :]
                            ).then_inc(a_w1.h, 16)
                        a_w1.n += 16 * EC
                        s.wait_ge(pe.h, p2_peB[(e, CAPC - 1)])
                        for h0 in range(0, HC, W2G):
                            s.dma_start(
                                bass.AP(w2sb, h0 * E,
                                        [[HC * E, 128], [E, W2G], [1, E]]),
                                bass.AP(w2b_d, ((e + 1) * HID + h0 * 128) * E,
                                        [[E, 128], [128 * E, W2G], [1, E]]),
                            ).then_inc(a_w2.h, 16)
                        a_w2.n += 16 * (HC // W2G)

        p2C = {}
        with nc.psum_tensor("cps", [1, E], F32) as cps:
            c_order = [NE - 1] + list(range(NE - 1))
            for idx, e in enumerate(c_order):
                buf = w2sb if idx % 2 == 0 else w1sb
                awbase = [0]

                @block.scalar
                def _(s, idx=idx, e=e, buf=buf):
                    if idx == 0:
                        return
                    if idx == 1:
                        s.wait_ge(pe.h, p2_peA[NE - 1])
                    else:
                        s.wait_ge(pe.h, p2C[idx - 2])
                    for h0 in range(0, HC, W2G):
                        s.dma_start(
                            bass.AP(buf, h0 * E,
                                    [[HC * E, 128], [E, W2G], [1, E]]),
                            bass.AP(w2b_d, (e * HID + h0 * 128) * E,
                                    [[E, 128], [128 * E, W2G], [1, E]]),
                        ).then_inc(a_c[idx % 2].h, 16)
                    a_c[idx % 2].n += 16 * (HC // W2G)

                @block.tensor
                def _(t, idx=idx, e=e, buf=buf):
                    if idx == 0:
                        t.wait_ge(pe.h, p2_peB[(NE - 1, CAPC - 1)])
                        ve.wait(t, p2_veh1[NE - 1])
                    else:
                        a_c[idx % 2].wait(t)
                    for h in range(HC):
                        for half in range(EH):
                            mm = t.matmul(
                                cps[0:1, half * EW:(half + 1) * EW],
                                rb1_bf[:, e * HC + h:e * HC + h + 1],
                                buf[:, h * E + half * EW: h * E + (half + 1) * EW],
                                start=(idx == 0 and h == 0), stop=False,
                                skip_group_check=True)
                    pe.inc(mm)
                    p2C[idx] = pe.n

            @block.tensor
            def _(t):
                g_pre.wait(t, 16)
                for half in range(EH):
                    mm = t.matmul(cps[0:1, half * EW:(half + 1) * EW],
                                  ones8[:, 0:1],
                                  b2_bf[:, half * EW:(half + 1) * EW],
                                  start=False, stop=True,
                                  skip_group_check=True)
                pe.inc(mm)

            @block.vector
            def _(v):
                cv = Chain(v, ve)
                pe.wait(v)
                cv.tensor_copy(csb[:, :], cps[0:1, :])

        with nc.psum_tensor("cbps", [128, E], F32) as cbps:
            @block.tensor
            def _(t):
                ve.wait(t)
                for half in range(EH):
                    mm = t.matmul(cbps[:, half * EW:(half + 1) * EW],
                                  ones1[:, :], csb[0:1, half * EW:(half + 1) * EW],
                                  start=True, stop=True)
                pe.inc(mm)

            @block.vector
            def _(v):
                cv = Chain(v, ve)
                pe.wait(v)
                cv.tensor_copy(cbc[:, :], cbps[:, :])

        if debug:
            for tc in range(NTC):
                @block.gpsimd
                def _(g, tc=tc):
                    if tc == 0:
                        s_zg[0].wait(g)
                        s_zg[1].wait(g)
                        ve.wait(g)
                    s_pre.wait(g)
                    g.indirect_dma_start(
                        zsc[0], None,
                        zg_d[:, :],
                        bass.IndirectOffsetOnAxis(dsti[:, tc:tc + 1], 0),
                    ).then_inc(g_pre.h, 16)
                    g_pre.n += 16

                @block.sync
                def _(sy, tc=tc):
                    g_pre.wait(sy)
                    sy.dma_start(dz_d[tc * 128:(tc + 1) * 128, :],
                                 zsc[0]).then_inc(s_pre.h, 16)
                    s_pre.n += 16

            @block.sync
            def _(sy):
                s_zg[0].wait(sy)
                s_zg[1].wait(sy)
                sy.dma_start(dxs_d[:, :], xs_d[:, :]).then_inc(s_pre.h, 16)
                ve.wait(sy)
                sy.dma_start(dm_d[:, :], m_all[:, :]).then_inc(s_pre.h, 16)
                sy.dma_start(dc_d[:, :], cbc[:, :]).then_inc(s_pre.h, 16)
                s_pre.n += 48

        p2es.close()
        wes.close()
        jbc = sb("jbc", [128, T], F32)
        k_fm = sb("k_fm", [128, EC * T], BF16)
        v_tm = sb("v_tm", [128, (T // 128) * E], BF16)
        q_fm = sb("q_fm", [128, EC * T], BF16)
        o_fm = sb("o_fm", [128, EC * T], BF16)
        wbuf = sb("wbuf", [128, EC * E], BF16)
        ab_sb = sb("ab_sb", [8, E], BF16)
        bb_sb = sb("bb_sb", [128, EC * R], BF16)
        u_sb = sb("u_sb", [8, T], BF16)

        p3_pe = {}
        p3_pe[(-1, 'end')] = pe.n
        p3_uve = {}
        p3_sv = {}
        prev_sv_last = [None]

        @block.gpsimd
        def _(g):
            cg = Chain(g, ge)
            g.wait_ge(pe.h, p3_pe[(-1, 'end')])
            cg.iota(jbc[:, :], pattern=[[1, T]], base=0, channel_multiplier=0,
                    allow_small_or_imprecise_dtypes=True)
        ge_jbc = ge.n

        for sq in range(NSEQ):
            t0 = sq * T
            pAes = ExitStack()
            xn3 = pAes.enter_context(
                nc.sbuf_tensor("xn3_" + str(sq), [128, EC * T], BF16))

            @block.sync
            def _(sy, sq=sq, t0=t0, xn3=xn3):
                if sq > 0:
                    pe.wait(sy)
                    sy.wait_ge(ve.h, prev_sv_last[0])
                    s_out[0].wait(sy)
                    s_out[1].wait(sy)
                else:
                    s_xsw[0].wait(sy)
                    s_xsw[1].wait(sy)
                    sy.wait_ge(pe.h, p3_pe[(-1, 'end')])
                    if debug:
                        s_pre.wait(sy)
                        g_pre.wait(sy)
                for c in range(EC):
                    sy.dma_start(xn3[:, c * T:(c + 1) * T],
                                 xs_d[c * 128:(c + 1) * 128, t0:t0 + T]
                                 ).then_inc(s_xn3.h, 16)
                s_xn3.n += 16 * EC

            def emit_u(bd, sq, xn3, first):
                @block.gpsimd
                def _(g):
                    w = p3_pe.get((sq, 'wfree'),
                                  p3_pe.get((sq - 1, 'end'), 0))
                    if w:
                        g.wait_ge(pe.h, w)
                    g.dma_start(
                        bb_sb[:, :],
                        bass.AP(bd, 0, [[R, 128], [128 * R, EC], [1, R]]),
                    ).then_inc(g_bb.h, 16)
                    g_bb.n += 16

                with nc.psum_tensor("ups_u" + str(sq) + str(g_bb.n),
                                    [8, T], F32) as ups:
                    @block.tensor
                    def _(t):
                        g_bb.wait(t)
                        s_xn3.wait(t)
                        ae.wait(t)
                        if first:
                            ve.wait(t)
                        if sq > 0 and prev_sv_last[0] is not None:
                            t.wait_ge(ve.h, prev_sv_last[0])
                        w = p3_pe.get((sq, 'wfree'), 0)
                        if w:
                            t.wait_ge(ve.h, p3_uve[(sq, 'last')])
                        for c in range(EC):
                            for nb in range(NPB):
                                mm = t.matmul(
                                    ups[:, nb * PB:(nb + 1) * PB],
                                    bb_sb[:, c * R:(c + 1) * R],
                                    xn3[:, c * T + nb * PB:
                                        c * T + (nb + 1) * PB],
                                    start=(c == 0), stop=(c == EC - 1))
                        pe.inc(mm)

                    @block.vector
                    def _(v):
                        cv = Chain(v, ve)
                        pe.wait(v)
                        cv.tensor_scalar(u_sb[:, :], ups[:, :], LORA_S, None,
                                         ALU.mult)
                        p3_uve[(sq, 'last')] = ve.n

            def proj_fm(wT_d, aT_d, bT_d, dest, scale, sq, tag, xn3,
                        first=False):
                emit_u(bT_d, sq, xn3, first)

                @block.gpsimd
                def _(g):
                    w = p3_pe.get((sq, 'wfree'),
                                  p3_pe.get((sq - 1, 'end'), 0))
                    if w:
                        g.wait_ge(pe.h, w)
                    for c in range(EC):
                        g.dma_start(wbuf[:, c * E:(c + 1) * E],
                                    wT_d[c * 128:(c + 1) * 128, :]
                                    ).then_inc(g_wf.h, 16)
                    g_wf.n += 16 * EC
                    g.dma_start(ab_sb[:, :], aT_d[:, :]).then_inc(g_ab.h, 16)
                    g_ab.n += 16

                with nc.psum_tensor("psA_" + tag + str(sq), [128, 4 * 512],
                                    F32) as psA:
                    for ic in range(EC):
                        @block.tensor
                        def _(t, ic=ic):
                            g_wf.wait(t)
                            g_ab.wait(t)
                            ve.wait(t, p3_uve[(sq, 'last')])
                            ae.wait(t)
                            for ec in range(EC):
                                for nb in range(NPB):
                                    t.matmul(
                                        psA[:, (nb % 4) * 512:
                                            (nb % 4) * 512 + PB],
                                        wbuf[:, ec * E + ic * 128:
                                             ec * E + ic * 128 + 128],
                                        xn3[:, ec * T + nb * PB:
                                            ec * T + (nb + 1) * PB],
                                        start=(ec == 0), stop=False)
                            for nb in range(NPB):
                                mm = t.matmul(
                                    psA[:, (nb % 4) * 512:(nb % 4) * 512 + PB],
                                    ab_sb[0:8, ic * 128:(ic + 1) * 128],
                                    u_sb[0:8, nb * PB:(nb + 1) * PB],
                                    start=False, stop=True)
                            pe.inc(mm)

                        @block.scalar
                        def _(s, ic=ic):
                            ca = Chain(s, ae)
                            pe.wait(s)
                            for nb in range(NPB):
                                ca.activation(
                                    dest[:, ic * T + nb * PB:
                                         ic * T + (nb + 1) * PB],
                                    psA[:, (nb % 4) * 512:(nb % 4) * 512 + PB],
                                    AF.Copy, scale=scale)
                p3_pe[(sq, 'wfree')] = pe.n

            proj_fm(wkT_d, akT_d, bkT_d, k_fm, 1.0, sq, "k", xn3,
                    first=(sq == 0))

            emit_u(bvT_d, sq, xn3, False)

            @block.gpsimd
            def _(g, sq=sq):
                g.wait_ge(pe.h, p3_pe[(sq, 'wfree')])
                for c in range(EC):
                    g.dma_start(wbuf[:, c * E:(c + 1) * E],
                                wvT_d[c * 128:(c + 1) * 128, :]
                                ).then_inc(g_wf.h, 16)
                g_wf.n += 16 * EC
                g.dma_start(ab_sb[:, :], avT_d[:, :]).then_inc(g_ab.h, 16)
                g_ab.n += 16

            with nc.psum_tensor("psV" + str(sq), [128, E], F32) as psV:
                for tc in range(T // 128):
                    @block.tensor
                    def _(t, tc=tc, sq=sq):
                        g_wf.wait(t)
                        g_ab.wait(t)
                        ve.wait(t, p3_uve[(sq, 'last')])
                        ae.wait(t)
                        for ec in range(EC):
                            for half in range(EH):
                                t.matmul(
                                    psV[:, half * EW:(half + 1) * EW],
                                    xn3[:, ec * T + tc * 128:
                                        ec * T + tc * 128 + 128],
                                    wbuf[:, ec * E + half * EW:
                                         ec * E + (half + 1) * EW],
                                    start=(ec == 0), stop=False)
                        for half in range(EH):
                            mm = t.matmul(
                                psV[:, half * EW:(half + 1) * EW],
                                u_sb[0:8, tc * 128: tc * 128 + 128],
                                ab_sb[0:8, half * EW:(half + 1) * EW],
                                start=False, stop=True)
                        pe.inc(mm)

                    @block.scalar
                    def _(s, tc=tc):
                        ca = Chain(s, ae)
                        pe.wait(s)
                        ca.activation(v_tm[:, tc * E:(tc + 1) * E],
                                      psV[:, :], AF.Copy)
            p3_pe[(sq, 'wfree')] = pe.n

            proj_fm(wqT_d, aqT_d, bqT_d, q_fm, SCL, sq, "q", xn3)
            proj_ae_done = ae.n

            @block.gpsimd
            def _(g, sq=sq):
                g.wait_ge(pe.h, p3_pe[(sq, 'wfree')])
                for c in range(EC):
                    g.dma_start(wbuf[:, c * E:(c + 1) * E],
                                wpT_d[c * 128:(c + 1) * 128, :]
                                ).then_inc(g_wf.h, 16)
                g_wf.n += 16 * EC

            pAes.close()
            pB1 = ExitStack()

            def sb3(name, shape, dt):
                return pB1.enter_context(
                    nc.sbuf_tensor(name + "_" + str(sq), shape, dt))

            colfac = sb3("colfac", [128, T], BF16)
            praw = [sb3("praw0", [128, T], BF16), sb3("praw1", [128, T], BF16)]
            pnrm = [sb3("pnrm0", [128, T], BF16), sb3("pnrm1", [128, T], BF16)]
            pt_sb = sb3("pt_sb", [128, T], BF16)

            with nc.psum_tensor("sbig" + str(sq), [128, SW], F32) as sbig, \
                 nc.psum_tensor("ptp0" + str(sq), [128, 128], BF16) as ptp0, \
                 nc.psum_tensor("ptp1" + str(sq), [128, 128], BF16) as ptp1, \
                 nc.psum_tensor("ops0" + str(sq), [64, 128], F32) as ops0, \
                 nc.psum_tensor("ops1" + str(sq), [64, 128], F32) as ops1:
                ptp = [ptp0, ptp1]
                ops = [ops0, ops1]
                iters = []
                last_user = {0: None, 1: None}
                it = 0
                for h in range(H):
                    for qt in range(QTN):
                        q0 = qt * 128
                        W = (qt + 1) * 128
                        covS = min(((q0 + 128 + 511) // 512) * 512, T)
                        small = covS <= SW // 2
                        soff = (SW // 2) * (it % 2) if small else 0
                        regions = [it % 2] if small else [0, 1]
                        deps = [last_user[r] for r in regions
                                if last_user[r] is not None]
                        dep = max(deps) if deps else None
                        for r in regions:
                            last_user[r] = it
                        iters.append(dict(h=h, qt=qt, q0=q0, W=W, covS=covS,
                                          soff=soff, dep=dep))
                        it += 1
                NIT = len(iters)
                p3_exp = {}
                p3_oev = {}
                p3_pnm = {}
                p3_rmx = {}
                p3_tr = {}
                ae_base = ae.n
                cnt = ae_base
                for i in range(NIT):
                    if iters[i]["qt"] == 0:
                        cnt += 1
                    cnt += 1
                    p3_exp[i] = cnt
                ae_hloop_end = cnt

                def emit_S(i):
                    d = iters[i]

                    @block.tensor
                    def _(t, i=i, d=d, sq=sq, proj_ae_done=proj_ae_done):
                        if d["dep"] is not None:
                            t.wait_ge(ae.h, p3_exp[d["dep"]])
                        elif sq > 0 and prev_sv_last[0] is not None:
                            t.wait_ge(ve.h, prev_sv_last[0])
                        if i == 0:
                            t.wait_ge(ae.h, proj_ae_done)
                        hb = (d["h"] % 2) * 64
                        hk = d["h"] // 2
                        nkc = (d["covS"] + 511) // 512
                        for nk in range(nkc):
                            n0 = nk * 512
                            n1 = min(n0 + 512, d["covS"])
                            mm = t.matmul(
                                sbig[:, d["soff"] + n0: d["soff"] + n1],
                                q_fm[hb:hb + 64,
                                     hk * T + d["q0"]: hk * T + d["q0"] + 128],
                                k_fm[hb:hb + 64, hk * T + n0: hk * T + n1],
                                start=True, stop=True)
                        pe.inc(mm)

                def emit_maskmax(i):
                    d = iters[i]

                    @block.vector
                    def _(v, i=i, d=d):
                        cv = Chain(v, ve)
                        pe.wait(v)
                        if i >= 1:
                            v.wait_ge(ae.h, p3_exp[i - 1])
                        cv.tensor_tensor(
                            sbig[:, d["soff"] + d["q0"]: d["soff"] + d["q0"] + 128],
                            mtri[:, :],
                            sbig[:, d["soff"] + d["q0"]: d["soff"] + d["q0"] + 128],
                            ALU.add)
                        cv.reduce_max(lnc[:, 0:1],
                                      sbig[:, d["soff"]:d["soff"] + d["W"]],
                                      axis=AXL.X, negate=True)
                        p3_rmx[i] = ve.n

                emit_S(0)
                emit_maskmax(0)

                for i in range(NIT):
                    d = iters[i]
                    qt = d["qt"]

                    if i + 1 < NIT:
                        emit_S(i + 1)
                        emit_maskmax(i + 1)

                    @block.vector
                    def _(v, i=i, d=d):
                        cv = Chain(v, ve)
                        v.wait_ge(ae.h, p3_exp[i])
                        if i >= 2:
                            v.wait_ge(pe.h, p3_tr[i - 2])
                        cv.scalar_tensor_tensor(pnrm[i % 2][:, 0:d["W"]],
                                                praw[i % 2][:, 0:d["W"]], 1.0,
                                                colfac[:, 0:d["W"]],
                                                ALU.mult, ALU.mult,
                                                accum_out=lnc[:, 1:2])
                        cv.reciprocal(lnc[:, 2:3], lnc[:, 1:2])
                        cv.tensor_scalar(pnrm[i % 2][:, 0:d["W"]],
                                         pnrm[i % 2][:, 0:d["W"]],
                                         lnc[:, 2:3], None, ALU.mult)
                        p3_pnm[i] = ve.n

                    @block.tensor
                    def _(t, i=i, qt=qt):
                        ve.wait(t, p3_pnm[i])
                        vb = ve.n
                        for kc in range(qt + 1):
                            w = vb + kc - 1
                            if w > p3_pnm[i]:
                                t.wait_ge(ve.h, w)
                            pe.inc(t.transpose(
                                ptp[kc % 2][:, :],
                                pnrm[i % 2][:, kc * 128:(kc + 1) * 128],
                                ident_bf[:, :]))
                        p3_tr[i] = pe.n

                    @block.vector
                    def _(v, i=i, qt=qt):
                        cv = Chain(v, ve)
                        pb = pe.n - (qt + 1)
                        for kc in range(qt + 1):
                            v.wait_ge(pe.h, pb + kc + 1)
                            cv.tensor_copy(pt_sb[:, kc * 128:(kc + 1) * 128],
                                           ptp[kc % 2][:, :])

                    @block.tensor
                    def _(t, i=i, qt=qt, d=d):
                        vb = ve.n - (qt + 1)
                        if i >= 2:
                            t.wait_ge(ve.h, p3_oev[i - 2])
                        for kc in range(qt + 1):
                            t.wait_ge(ve.h, vb + kc + 1)
                            mm = t.matmul(
                                ops[i % 2][:, :],
                                v_tm[:, kc * E + d["h"] * DH:
                                     kc * E + (d["h"] + 1) * DH],
                                pt_sb[:, kc * 128:(kc + 1) * 128],
                                start=(kc == 0), stop=(kc == qt))
                        pe.inc(mm)

                    @block.vector
                    def _(v, i=i, qt=qt, d=d):
                        cv = Chain(v, ve)
                        pe.wait(v)
                        hb = (d["h"] % 2) * 64
                        hk = d["h"] // 2
                        cv.tensor_copy(
                            o_fm[hb:hb + 64,
                                 hk * T + qt * 128: hk * T + qt * 128 + 128],
                            ops[i % 2][:, :])
                        p3_oev[i] = ve.n

                @block.scalar
                def _(s):
                    ca = Chain(s, ae)
                    for i in range(NIT):
                        d = iters[i]
                        if d["qt"] == 0:
                            h = d["h"]
                            ge.wait(s, ge_jbc)
                            if i >= 1:
                                s.wait_ge(ve.h, p3_pnm[i - 1])
                            ca.activation(colfac[:, :], jbc[:, :], AF.Exp,
                                          scale=-slopes[h])
                        s.wait_ge(ve.h, p3_rmx[i])
                        if i >= 2:
                            s.wait_ge(ve.h, p3_pnm[i - 2])
                        ca.activation(praw[i % 2][:, 0:d["W"]],
                                      sbig[:, d["soff"]:d["soff"] + d["W"]],
                                      AF.Exp, bias=lnc[:, 0:1])
                    assert ae.n == ae_hloop_end, (ae.n, ae_hloop_end)

                pB1.close()
                pB2 = ExitStack()

                def sb4(name, shape, dt):
                    return pB2.enter_context(
                        nc.sbuf_tensor(name + "_" + str(sq), shape, dt))

                ztb = [sb4("ztb0", [128, E], BF16), sb4("ztb1", [128, E], BF16)]
                xtb = [sb4("xtb0", [128, E], F32), sb4("xtb1", [128, E], F32)]
                gts = sb4("gts", [128, E], BF16)
                otb = [sb4("otb0", [128, E], F32), sb4("otb1", [128, E], F32)]

                for gi0 in range(min(2, QTN)):
                    @block.gpsimd
                    def _(g, gi0=gi0, sq=sq, t0=t0):
                        s_zg[0].wait(g)
                        s_zg[1].wait(g)
                        pe.wait(g)
                        ae.wait(g)
                        ve.wait(g)
                        g.indirect_dma_start(
                            ztb[gi0 % 2][:, :], None,
                            zg_d[:, :],
                            bass.IndirectOffsetOnAxis(
                                dsti[:, t0 // 128 + gi0: t0 // 128 + gi0 + 1], 0),
                        ).then_inc(g_zi[gi0 % 2].h, 16)
                        g_zi[gi0 % 2].n += 16

                    @block.sync
                    def _(sy, gi0=gi0, sq=sq, t0=t0):
                        pe.wait(sy)
                        sy.dma_start(
                            xtb[gi0 % 2][:, :],
                            x_d[t0 + gi0 * 128: t0 + (gi0 + 1) * 128, :]
                        ).then_inc(s_xe[gi0 % 2].h, 16)
                        s_xe[gi0 % 2].n += 16

                for qt in range(QTN):
                    @block.tensor
                    def _(t, qt=qt, sq=sq):
                        g_wf.wait(t)
                        if qt == 0:
                            ve.wait(t)
                        if qt >= 2:
                            ve.wait(t, p3_sv[(sq, qt - 2)])
                        for dc in range(EC):
                            for half in range(EH):
                                mm = t.matmul(
                                    sbig[:, (qt % 2) * E + half * EW:
                                         (qt % 2) * E + (half + 1) * EW],
                                    o_fm[:, dc * T + qt * 128:
                                         dc * T + qt * 128 + 128],
                                    wbuf[:, dc * E + half * EW:
                                         dc * E + (half + 1) * EW],
                                    start=(dc == 0), stop=(dc == EC - 1))
                        pe.inc(mm)

                    @block.vector
                    def _(v, qt=qt, sq=sq):
                        cv = Chain(v, ve)
                        bi = qt % 2
                        pe.wait(v)
                        g_zi[bi].wait(v)
                        gidx = sq * QTN + qt
                        if gidx >= 2:
                            v.wait_ge(s_out[gidx % 2].h, 16 * (gidx // 2))
                        cv.tensor_tensor(otb[bi][:, :], ztb[bi][:, :],
                                         cbc[:, :], ALU.add)

                    @block.scalar
                    def _(s, qt=qt):
                        ca = Chain(s, ae)
                        ve.wait(s)
                        ca.activation(gts[:, :], otb[qt % 2][:, :], AF.Silu)

                    @block.vector
                    def _(v, qt=qt, sq=sq):
                        cv = Chain(v, ve)
                        bi = qt % 2
                        ae.wait(v)
                        s_xe[bi].wait(v)
                        cv.tensor_tensor(otb[bi][:, :],
                                         sbig[:, bi * E:(bi + 1) * E],
                                         gts[:, :], ALU.mult)
                        cv.tensor_tensor(otb[bi][:, :], otb[bi][:, :],
                                         xtb[bi][:, :], ALU.add)
                        p3_sv[(sq, qt)] = ve.n

                    @block.sync
                    def _(sy, qt=qt, sq=sq, t0=t0):
                        bi = qt % 2
                        sy.wait_ge(ve.h, p3_sv[(sq, qt)])
                        sy.dma_start(
                            out_d[t0 + qt * 128: t0 + (qt + 1) * 128, :],
                            otb[bi][:, :]).then_inc(s_out[(sq * QTN + qt) % 2].h, 16)
                        s_out[(sq * QTN + qt) % 2].n += 16
                        gi2 = qt + 2
                        if gi2 < QTN:
                            sy.dma_start(
                                xtb[gi2 % 2][:, :],
                                x_d[t0 + gi2 * 128: t0 + (gi2 + 1) * 128, :]
                            ).then_inc(s_xe[gi2 % 2].h, 16)
                            s_xe[gi2 % 2].n += 16

                    @block.gpsimd
                    def _(g, qt=qt, sq=sq, t0=t0):
                        gi2 = qt + 2
                        if gi2 < QTN:
                            g.wait_ge(ve.h, p3_sv[(sq, qt)])
                            g.indirect_dma_start(
                                ztb[gi2 % 2][:, :], None,
                                zg_d[:, :],
                                bass.IndirectOffsetOnAxis(
                                    dsti[:, t0 // 128 + gi2: t0 // 128 + gi2 + 1], 0),
                            ).then_inc(g_zi[gi2 % 2].h, 16)
                            g_zi[gi2 % 2].n += 16

                prev_sv_last[0] = p3_sv[(sq, QTN - 1)]
            p3_pe[(sq, 'end')] = pe.n
            pB2.close()

            if debug and sq == NSEQ - 1:
                @block.gpsimd
                def _(g):
                    ae.wait(g)
                    ve.wait(g)
                    for (dd, ss) in ((dq_d, q_fm), (dk_d, k_fm), (dv_d, v_tm),
                                     (do_d, o_fm)):
                        g.dma_start(dd[:, :], ss[:, :]).then_inc(g_pre.h, 16)
                        g_pre.n += 16

        @block.sync
        def _(sy):
            sy.wait_ge(s_out[0].h, s_out[0].n)
            sy.wait_ge(s_out[1].h, s_out[1].n)
            if debug:
                sy.wait_ge(s_pre.h, s_pre.n)
                sy.wait_ge(g_pre.h, g_pre.n)

    return nc


_NC_CACHE = {}


def _get_nc(key, cfg, debug=False):
    if key not in _NC_CACHE:
        _NC_CACHE[key] = build(cfg, debug=debug)
    return _NC_CACHE[key]


def make_in_maps(inputs, cfg, ncore=8):
    E, T, NSEQ, HID = cfg["E"], cfg["T"], cfg["NSEQ"], cfg["HID"]
    x = np.asarray(inputs["L0_attn_x"], np.float32)
    B = x.shape[0]
    per = B // ncore
    assert per * ncore == B and per == NSEQ

    def t(a):
        return np.ascontiguousarray(np.asarray(a, np.float32).T)

    com = {
        "wqT": t(inputs["L0_attn_Wq"]), "wkT": t(inputs["L0_attn_Wk"]),
        "wvT": t(inputs["L0_attn_Wv"]), "wpT": t(inputs["L0_attn_Wp"]),
        "bqT": t(inputs["L0_attn_Bq"]), "bkT": t(inputs["L0_attn_Bk"]),
        "bvT": t(inputs["L0_attn_Bv"]),
        "aqT": t(inputs["L0_attn_Aq"]), "akT": t(inputs["L0_attn_Ak"]),
        "avT": t(inputs["L0_attn_Av"]),
        "w1": np.ascontiguousarray(
            np.asarray(inputs["L0_ffn_W1"], np.float32).reshape(-1, HID)),
        "b1": np.asarray(inputs["L0_ffn_b1"], np.float32),
        "w2": np.ascontiguousarray(
            np.asarray(inputs["L0_ffn_W2"], np.float32).reshape(-1, E)),
        "b2": np.asarray(inputs["L0_ffn_b2"], np.float32),
        "wr": np.asarray(inputs["L0_ffn_Wr"], np.float32),
        "br": np.ascontiguousarray(
            np.asarray(inputs["L0_ffn_br"], np.float32).reshape(-1, 1)),
    }
    in_maps = []
    for c in range(ncore):
        m = dict(com)
        m["x"] = np.ascontiguousarray(
            x[c * per:(c + 1) * per].reshape(per * T, E))
        in_maps.append(m)
    return in_maps


def kernel(**inputs):
    cfg = CFG_FULL
    NCORE = 8
    T, E = cfg["T"], cfg["E"]
    in_maps = make_in_maps(inputs, cfg, NCORE)
    per = cfg["NSEQ"]
    nc = _get_nc("full", cfg)
    res = run_bass_kernel_spmd(nc, in_maps, core_ids=list(range(NCORE)))
    out = np.concatenate(
        [np.asarray(r["out"]).reshape(per, T, E) for r in res.results], axis=0)
    return out.astype(np.float32)


# version 17
# speedup vs baseline: 1.0219x; 1.0219x over previous
import math
import numpy as np
import ml_dtypes
from contextlib import ExitStack
import concourse.bass as bass
import concourse.mybir as mybir
from concourse.bass_utils import run_bass_kernel_spmd

F32 = mybir.dt.float32
BF16 = mybir.dt.bfloat16
I32 = mybir.dt.int32
AF = mybir.ActivationFunctionType
ALU = mybir.AluOpType
AXL = mybir.AxisListType

NEG = -1.0e30

CFG_FULL = dict(E=1024, H=16, HID=4096, NE=8, T=2048, NSEQ=2, R=8)
CFG_SIM = dict(E=128, H=2, HID=512, NE=8, T=256, NSEQ=2, R=8)

_CHAIN_OPS = {
    "activation", "copy", "add", "mul", "sqrt", "square", "sign", "memset",
    "memzero", "tensor_copy", "tensor_tensor", "tensor_add", "tensor_sub",
    "tensor_mul", "tensor_max", "tensor_scalar", "tensor_single_scalar",
    "tensor_scalar_add", "tensor_scalar_sub", "tensor_scalar_mul",
    "tensor_scalar_max", "tensor_scalar_min", "tensor_relu", "tensor_reduce",
    "reduce_max", "reduce_sum", "reciprocal", "select", "copy_predicated",
    "tensor_tensor_reduce", "scalar_tensor_tensor", "tensor_tensor_scan",
    "iota", "bn_stats", "bn_aggr", "max", "max_index", "stream_shuffle",
    "affine_select",
}


class Ctr:

    def __init__(self, h):
        self.h = h
        self.n = 0

    def inc(self, instr, k=1):
        instr.then_inc(self.h, k)
        self.n += k
        return self.n

    def wait(self, eng, n=None):
        v = self.n if n is None else n
        if v > 0:
            eng.wait_ge(self.h, v)


class Chain:

    def __init__(self, eng, ctr):
        self._e = eng
        self._c = ctr

    def __getattr__(self, name):
        e = self._e
        a = getattr(e, name)
        if name not in _CHAIN_OPS:
            return a
        c = self._c

        def wrapped(*args, **kw):
            if c.n > 0:
                e.wait_ge(c.h, c.n)
            instr = a(*args, **kw)
            c.inc(instr)
            return instr

        return wrapped


def build(cfg, debug=False):
    E, H, HID, NE, T, NSEQ, R = (cfg["E"], cfg["H"], cfg["HID"], cfg["NE"],
                                 cfg["T"], cfg["NSEQ"], cfg["R"])
    DH = 64
    assert E == H * DH
    EC = E // 128
    HC = HID // 128
    NT = T * NSEQ
    NTC = NT // 128
    TBL = min(512, NT)
    NTB = NT // TBL
    TCB = TBL // 128
    QTN = T // 128
    PB = min(512, T)
    NPB = T // PB
    assert NPB <= 4
    EW = min(512, E)
    EH = E // EW
    SCL = 0.125
    LORA_S = 2.0
    SW = min(2 * T, 2048)
    CAP = 768 if T >= 2048 else 256
    CAPC = CAP // 128
    NCAP = NE * CAP
    CSTEPS = int(math.log2(NT))
    assert 2 ** CSTEPS == NT
    start = 2 ** (-2 ** (-(math.log2(H) - 3)))
    slopes = [start * start ** i for i in range(H)]
    W2G = 4 if HC % 4 == 0 else 1

    nc = bass.Bass("TRN2", target_bir_lowering=False)

    x_d = nc.dram_tensor("x", [NT, E], F32, kind="ExternalInput")
    out_d = nc.dram_tensor("out", [NT, E], F32, kind="ExternalOutput")
    wqT_d = nc.dram_tensor("wqT", [E, E], F32, kind="ExternalInput")
    wkT_d = nc.dram_tensor("wkT", [E, E], F32, kind="ExternalInput")
    wvT_d = nc.dram_tensor("wvT", [E, E], F32, kind="ExternalInput")
    wpT_d = nc.dram_tensor("wpT", [E, E], F32, kind="ExternalInput")
    bqT_d = nc.dram_tensor("bqT", [E, R], F32, kind="ExternalInput")
    bkT_d = nc.dram_tensor("bkT", [E, R], F32, kind="ExternalInput")
    bvT_d = nc.dram_tensor("bvT", [E, R], F32, kind="ExternalInput")
    aqT_d = nc.dram_tensor("aqT", [R, E], F32, kind="ExternalInput")
    akT_d = nc.dram_tensor("akT", [R, E], F32, kind="ExternalInput")
    avT_d = nc.dram_tensor("avT", [R, E], F32, kind="ExternalInput")
    w1_d = nc.dram_tensor("w1", [NE * E, HID], F32, kind="ExternalInput")
    b1_d = nc.dram_tensor("b1", [NE, HID], F32, kind="ExternalInput")
    w2_d = nc.dram_tensor("w2", [NE * HID, E], F32, kind="ExternalInput")
    b2_d = nc.dram_tensor("b2", [NE, E], F32, kind="ExternalInput")
    wr_d = nc.dram_tensor("wr", [E, NE], F32, kind="ExternalInput")
    br_d = nc.dram_tensor("br", [NE, 1], F32, kind="ExternalInput")

    xs_d = nc.dram_tensor("xs", [E, NT], BF16, kind="Internal")
    xt_d = nc.dram_tensor("xt", [NT, E], BF16, kind="Internal")
    xg_d = nc.dram_tensor("xg", [NCAP, E], BF16, kind="Internal")
    zg_d = nc.dram_tensor("zg", [NCAP, E], BF16, kind="Internal")
    w1b_d = nc.dram_tensor("w1b", [NE * E, HID], BF16, kind="Internal")
    w2b_d = nc.dram_tensor("w2b", [NE * HID, E], BF16, kind="Internal")
    if debug:
        dxs_d = nc.dram_tensor("d_xs", [E, NT], BF16, kind="ExternalOutput")
        dz_d = nc.dram_tensor("d_z", [NT, E], BF16, kind="ExternalOutput")
        dm_d = nc.dram_tensor("d_m", [128, NE * NTC], F32, kind="ExternalOutput")
        dc_d = nc.dram_tensor("d_c", [128, E], F32, kind="ExternalOutput")
        dq_d = nc.dram_tensor("d_q", [128, EC * T], F32, kind="ExternalOutput")
        dk_d = nc.dram_tensor("d_k", [128, EC * T], F32, kind="ExternalOutput")
        dv_d = nc.dram_tensor("d_v", [128, (T // 128) * E], F32,
                              kind="ExternalOutput")
        do_d = nc.dram_tensor("d_o", [128, EC * T], F32, kind="ExternalOutput")

    es = ExitStack()

    def sb(name, shape, dt):
        return es.enter_context(nc.sbuf_tensor(name, shape, dt))

    ident_i = sb("ident_i", [128, 128], F32)
    ident_bf = sb("ident_bf", [128, 128], BF16)
    ident_f = sb("ident_f", [128, 128], F32)
    mtri = sb("mtri", [128, 128], F32)
    b1_fm = sb("b1_fm", [128, NE * HC], F32)
    rb1_bf = sb("rb1_bf", [128, NE * HC], BF16)
    bneg_fm = sb("bneg_fm", [128, NE * HC], F32)
    nrb_fm = sb("nrb_fm", [128, NE * HC], F32)
    rb1_fm = nrb_fm
    m_all = sb("m_all", [128, NE * NTC], F32) if debug else None
    b2_bf = sb("b2_bf", [8, E], BF16)
    wr_sb = sb("wr_sb", [128, EC * NE], F32)
    wr_hi = sb("wr_hi", [128, EC * NE], BF16)
    wr_lo = sb("wr_lo", [128, EC * NE], BF16)
    br_col = sb("br_col", [8, 1], F32)
    ones1 = sb("ones1", [1, 128], BF16)
    ones8 = sb("ones8", [8, 1], BF16)
    csb = sb("csb", [1, E], BF16)
    cbc = sb("cbc", [128, E], F32)
    lg_sb = sb("lg_sb", [8, 128], F32)
    lgt = sb("lgt", [128, 8], F32)
    lnc = sb("lncols", [128, 8], F32)
    dsti = sb("dsti", [128, NTC], I32)
    ones8f = sb("ones8f", [8, 1], F32)
    eoff = sb("eoff", [8, 1], F32)
    capc = sb("capc", [8, 1], F32)
    wes = ExitStack()
    w1sb = wes.enter_context(nc.sbuf_tensor("w1sb", [128, EC * HID], BF16))
    w2sb = wes.enter_context(nc.sbuf_tensor("w2sb", [128, HC * E], BF16))

    def sem(name):
        return Ctr(es.enter_context(nc.semaphore(name)))

    s_pre = sem("s_pre")
    g_pre = sem("g_pre")
    s_xin = [sem("s_xin0"), sem("s_xin1")]
    s_xsw = [sem("s_xsw0"), sem("s_xsw1")]
    s_xg = [sem("s_xg0"), sem("s_xg1")]
    g_w1 = sem("g_w1")
    g_w2 = sem("g_w2")
    g_cw = sem("g_cw")
    a_w1 = sem("a_w1")
    a_w2 = sem("a_w2")
    a_c = [sem("a_c0"), sem("a_c1")]
    a_z0 = sem("a_z0")
    s_zg = [sem("s_zg0"), sem("s_zg1")]
    s_xt = sem("s_xt")
    g_sc = [sem("g_sc0"), sem("g_sc1")]
    g_zi = [sem("g_zi0"), sem("g_zi1")]
    g_wf = sem("g_wf")
    g_ab = sem("g_ab")
    g_bb = sem("g_bb")
    s_xn3 = sem("s_xn3")
    s_zi = [sem("s_zi0"), sem("s_zi1")]
    s_xe = [sem("s_xe0"), sem("s_xe1")]
    s_out = [sem("s_out0"), sem("s_out1")]
    pe = sem("pe")
    ve = sem("ve")
    ae = sem("ae")
    ge = sem("ge")

    block = es.enter_context(nc.Block())
    p1es = ExitStack()
    b1_sb = p1es.enter_context(nc.sbuf_tensor("b1_sb", [8, HID], F32))

    with es:
        def sb1(name, shape, dt):
            return p1es.enter_context(nc.sbuf_tensor(name, shape, dt))

        xbuf = [sb1("xbuf0", [128, E], F32), sb1("xbuf1", [128, E], F32)]
        xnf = [sb1("xnf0", [128, E], F32), sb1("xnf1", [128, E], F32)]
        xnh = sb1("xnh", [128, E], BF16)
        xnl = sb1("xnl", [128, E], BF16)
        stg = [sb1("stg0", [128, E], BF16), sb1("stg1", [128, E], BF16)]
        stl = sb1("stl", [128, E], BF16)
        mmt = sb1("mmt", [128, 8], F32)
        m8a = sb1("m8a", [8, NT], F32)
        assert HID >= NT
        c2b = b1_sb
        dsr = sb1("dsr", [1, 512], F32)

        @block.sync
        def _(sy):
            sy.dma_start(b1_sb[:, :], b1_d[:, :]).then_inc(s_pre.h, 16)
            sy.dma_start(br_col[:, :], br_d[:, :]).then_inc(s_pre.h, 16)
            sy.dma_start(wr_sb[:, :],
                         bass.AP(wr_d, 0, [[NE, 128], [128 * NE, EC], [1, NE]])
                         ).then_inc(s_pre.h, 16)
            s_pre.n = 48

        @block.gpsimd
        def _(g):
            cg = Chain(g, ge)
            cg.memset(stg[0][:, :], 0.0)
            cg.iota(ident_i[:, :], pattern=[[-1, 128]], base=0,
                    channel_multiplier=1, allow_small_or_imprecise_dtypes=True)
            cg.memset(ones1[:, :], 1.0)
            cg.memset(ones8[:, :], 1.0)
            cg.memset(ones8f[:, :], 1.0)
            cg.iota(eoff[:, :], pattern=[[0, 1]], base=-1, channel_multiplier=CAP,
                    allow_small_or_imprecise_dtypes=True)
            cg.iota(capc[:, :], pattern=[[0, 1]], base=CAP - 1,
                    channel_multiplier=CAP,
                    allow_small_or_imprecise_dtypes=True)
            g.dma_start(b2_bf[:, :], b2_d[:, :]).then_inc(g_pre.h, 16)
            g_pre.n += 16
            for e in range(NE):
                if e > 0:
                    g.wait_ge(g_cw.h, g_cw.n)
                g.dma_start(w1b_d[e * E:(e + 1) * E, :],
                            w1_d[e * E:(e + 1) * E, :]).then_inc(g_cw.h, 16)
                g_cw.n += 16
                g.wait_ge(g_cw.h, g_cw.n)
                g.dma_start(w2b_d[e * HID:(e + 1) * HID, :],
                            w2_d[e * HID:(e + 1) * HID, :]).then_inc(g_cw.h, 16)
                g_cw.n += 16

        @block.vector
        def _(v):
            cv = Chain(v, ve)
            ge.wait(v)
            cv.tensor_scalar(ident_bf[:, :], ident_i[:, :], 0.0, None,
                             ALU.is_equal)
            cv.tensor_scalar(ident_f[:, :], ident_i[:, :], 0.0, None,
                             ALU.is_equal)
            cv.tensor_scalar(mtri[:, :], ident_i[:, :], 0.0, NEG, ALU.is_lt,
                             ALU.mult)
            s_pre.wait(v)
            cv.tensor_copy(wr_hi[:, :], wr_sb[:, :])
            cv.tensor_tensor(wr_lo[:, :], wr_sb[:, :], wr_hi[:, :], ALU.subtract)

        @block.scalar
        def _(s):
            ge.wait(s, 1)
            for r0 in range(NCAP // 128):
                s.dma_start(xg_d[r0 * 128:(r0 + 1) * 128, :],
                            stg[0][:, :]).then_inc(a_z0.h, 16)
            a_z0.n += 16 * (NCAP // 128)
            s.wait_ge(g_cw.h, 32)
            for d in range(EC):
                s.dma_start(w1sb[:, d * HID:(d + 1) * HID],
                            w1b_d[d * 128:(d + 1) * 128, :]).then_inc(a_w1.h, 16)
            a_w1.n += 16 * EC
            for h0 in range(0, HC, W2G):
                s.dma_start(
                    bass.AP(w2sb, h0 * E, [[HC * E, 128], [E, W2G], [1, E]]),
                    bass.AP(w2b_d, h0 * 128 * E,
                            [[E, 128], [128 * E, W2G], [1, E]]),
                ).then_inc(a_w2.h, 16)
            a_w2.n += 16 * (HC // W2G)

        with nc.psum_tensor("ptr8", [128, 8], F32) as ptr8:
            for c in range(HC):
                @block.tensor
                def _(t, c=c):
                    ve.wait(t)
                    pe.inc(t.transpose(ptr8[:, :],
                                       b1_sb[0:8, c * 128:(c + 1) * 128],
                                       ident_f[0:8, 0:8]))

                @block.vector
                def _(v, c=c):
                    cv = Chain(v, ve)
                    pe.wait(v)
                    cv.tensor_copy(
                        bass.AP(b1_fm, c, [[NE * HC, 128], [HC, 8]]),
                        ptr8[:, :])

            @block.vector
            def _(v):
                cv = Chain(v, ve)
                cv.tensor_scalar(nrb_fm[:, :], b1_fm[:, :], 0.0, None, ALU.max)
                cv.tensor_copy(rb1_bf[:, :], nrb_fm[:, :])
                cv.tensor_scalar(bneg_fm[:, :], b1_fm[:, :], 0.0, None, ALU.min)
                cv.tensor_scalar(nrb_fm[:, :], nrb_fm[:, :], -1.0, None,
                                 ALU.mult)


        p1_sqrt = {}
        p1_hilo = {}
        p1_mmt_d = {}
        p1b_dst = [None]
        p1b_cur = [None]
        p1b_ve = {}
        p1_tr = {}
        p1_lg = {}
        p1_stage_d = {}
        IVE = 1.0 / E
        with nc.psum_tensor("pst0", [128, 128], BF16) as pst0, \
             nc.psum_tensor("pst1", [128, 128], BF16) as pst1, \
             nc.psum_tensor("lgps", [8, 128], F32) as lgps, \
             nc.psum_tensor("lgtp", [128, 8], F32) as lgtp, \
             nc.psum_tensor("m8ps", [8, 128], F32) as m8ps, \
             nc.psum_tensor("dstps", [1, 512], F32) as dstps, \
             nc.psum_tensor("dcps", [128, 1], F32) as dcps:
            p1_m8 = {}
            pst = [pst0, pst1]
            for j in range(min(2, NTC)):
                @block.sync
                def _(sy, j=j):
                    sy.dma_start(xbuf[j % 2][:, :],
                                 x_d[j * 128:(j + 1) * 128, :]
                                 ).then_inc(s_xin[j % 2].h, 16)
                    s_xin[j % 2].n += 16

            for j in range(NTC):
                bi = j % 2

                @block.vector
                def _(v, j=j, bi=bi):
                    cv = Chain(v, ve)
                    s_xin[bi].wait(v)
                    if j >= 1:
                        ae.wait(v, p1_sqrt[j - 1])
                    xt = xbuf[bi]
                    cv.scalar_tensor_tensor(xnf[bi][:, :], xt[:, :], 1.0,
                                            xt[:, :], ALU.mult, ALU.mult,
                                            accum_out=lnc[:, 1:2])
                    cv.reduce_sum(lnc[:, 0:1], xt[:, :], axis=AXL.X)
                    cv.tensor_scalar(lnc[:, 2:3], lnc[:, 0:1], IVE, None,
                                     ALU.mult)
                    cv.tensor_scalar(lnc[:, 3:4], lnc[:, 1:2], IVE, None,
                                     ALU.mult)
                    cv.scalar_tensor_tensor(lnc[:, 4:5], lnc[:, 2:3],
                                            lnc[:, 2:3], lnc[:, 3:4],
                                            ALU.mult, ALU.subtract)
                    cv.tensor_scalar(lnc[:, 5:6], lnc[:, 4:5], -1.0, 1e-5,
                                     ALU.mult, ALU.add)

                @block.scalar
                def _(s, j=j):
                    ca = Chain(s, ae)
                    ve.wait(s)
                    ca.activation(lnc[:, 6:7], lnc[:, 5:6], AF.Sqrt)
                    p1_sqrt[j] = ae.n

                @block.vector
                def _(v, j=j, bi=bi):
                    cv = Chain(v, ve)
                    ae.wait(v, p1_sqrt[j])
                    if j >= 1:
                        pe.wait(v, p1_tr[j - 1])
                        s_xt.wait(v)
                    cv.reciprocal(lnc[:, 7:8], lnc[:, 6:7])
                    cv.scalar_tensor_tensor(lnc[:, 4:5], lnc[:, 2:3], -1.0,
                                            lnc[:, 7:8], ALU.mult, ALU.mult)
                    cv.tensor_scalar(xnf[bi][:, :], xbuf[bi][:, :],
                                     lnc[:, 7:8], lnc[:, 4:5], ALU.mult,
                                     ALU.add)
                    cv.tensor_copy(xnh[:, :], xnf[bi][:, :])
                    cv.tensor_tensor(xnl[:, :], xnf[bi][:, :], xnh[:, :],
                                     ALU.subtract)
                    p1_hilo[j] = ve.n

                @block.sync
                def _(sy, j=j):
                    sy.wait_ge(ve.h, p1_hilo[j])
                    sy.dma_start(xt_d[j * 128:(j + 1) * 128, :],
                                 xnh[:, :]).then_inc(s_xt.h, 16)
                    s_xt.n += 16
                    if j + 2 < NTC:
                        sy.dma_start(xbuf[j % 2][:, :],
                                     x_d[(j + 2) * 128:(j + 3) * 128, :]
                                     ).then_inc(s_xin[j % 2].h, 16)
                        s_xin[j % 2].n += 16

                @block.tensor
                def _(t, j=j):
                    ve.wait(t, p1_hilo[j])
                    if j >= 2:
                        s_xsw[j % 2].wait(t)
                    ab = ae.n
                    for c in range(EC):
                        w = ab + c - 1
                        if w >= 1:
                            t.wait_ge(ae.h, w)
                        pe.inc(t.transpose(pst[c % 2][:, :],
                                           xnh[:, c * 128:(c + 1) * 128],
                                           ident_bf[:, :]))
                    for c in range(EC):
                        if c < 2:
                            c2 = EC - 1 if (EC - 1) % 2 == c % 2 else EC - 2
                            w = ab + c2 + 1 if c2 >= 0 else ab
                        else:
                            w = ab + EC + c - 1
                        if w >= 1:
                            t.wait_ge(ae.h, w)
                        pe.inc(t.transpose(pst[c % 2][:, :],
                                           xnl[:, c * 128:(c + 1) * 128],
                                           ident_bf[:, :]))
                    p1_tr[j] = pe.n

                @block.scalar
                def _(s, j=j):
                    ca = Chain(s, ae)
                    if j == 0:
                        a_z0.wait(s)
                    base = pe.n - 2 * EC
                    for c in range(EC):
                        s.wait_ge(pe.h, base + c + 1)
                        ca.activation(stg[j % 2][:, c * 128:(c + 1) * 128],
                                      pst[c % 2][:, :], AF.Copy)
                    for c in range(EC):
                        s.wait_ge(pe.h, base + EC + c + 1)
                        ca.activation(stl[:, c * 128:(c + 1) * 128],
                                      pst[c % 2][:, :], AF.Copy)
                    p1_stage_d[j] = ae.n

                @block.tensor
                def _(t, j=j):
                    ae.wait(t, p1_stage_d[j])
                    if j >= 1:
                        ve.wait(t, p1_lg[j - 1])
                    first = True
                    for c in range(EC):
                        t.matmul(lgps[:, :], wr_hi[:, c * NE:(c + 1) * NE],
                                 stg[j % 2][:, c * 128:(c + 1) * 128],
                                 start=first, stop=False)
                        first = False
                        t.matmul(lgps[:, :], wr_lo[:, c * NE:(c + 1) * NE],
                                 stg[j % 2][:, c * 128:(c + 1) * 128],
                                 start=False, stop=False)
                        mm = t.matmul(lgps[:, :], wr_hi[:, c * NE:(c + 1) * NE],
                                      stl[:, c * 128:(c + 1) * 128],
                                      start=False, stop=(c == EC - 1))
                    pe.inc(mm)

                @block.vector
                def _(v, j=j):
                    cv = Chain(v, ve)
                    pe.wait(v)
                    cv.tensor_scalar(lg_sb[:, :], lgps[:, :], br_col[:, 0:1],
                                     None, ALU.add)
                    p1_lg[j] = ve.n

                @block.tensor
                def _(t, j=j):
                    ve.wait(t, p1_lg[j])
                    pe.inc(t.transpose(lgtp[:, :], lg_sb[:, :],
                                       ident_f[0:8, 0:8]))

                @block.vector
                def _(v, j=j):
                    cv = Chain(v, ve)
                    pe.wait(v)
                    cv.tensor_copy(lgt[:, :], lgtp[:, :])
                    cv.reduce_max(lnc[:, 0:1], lgt[:, :], axis=AXL.X)
                    cv.tensor_scalar(mmt[:, :], lgt[:, :], lnc[:, 0:1], None,
                                     ALU.is_ge)
                    p1_mmt = ve.n
                    if debug:
                        cv.tensor_scalar(
                            bass.AP(m_all, j, [[NE * NTC, 128], [NTC, NE]]),
                            lgt[:, :], lnc[:, 0:1], None, ALU.is_ge)
                    p1_mmt_d[j] = p1_mmt

                @block.tensor
                def _(t, j=j):
                    ve.wait(t, p1_mmt_d[j])
                    if j >= 1:
                        ve.wait(t, p1_m8[j - 1])
                    pe.inc(t.transpose(m8ps[:, :], mmt[:, :],
                                       ident_f[:, :]))

                @block.vector
                def _(v, j=j):
                    cv = Chain(v, ve)
                    pe.wait(v)
                    cv.tensor_copy(m8a[0:8, j * 128:(j + 1) * 128], m8ps[:, :])
                    p1_m8[j] = ve.n

                @block.sync
                def _(sy, j=j):
                    sy.wait_ge(ae.h, p1_stage_d[j])
                    sy.dma_start(
                        bass.AP(xs_d, j * 128,
                                [[NT, 128], [128 * NT, EC], [1, 128]]),
                        bass.AP(stg[j % 2], 0, [[E, 128], [128, EC], [1, 128]]),
                    ).then_inc(s_xsw[j % 2].h, 16)
                    s_xsw[j % 2].n += 16


            @block.vector
            def _(v):
                cv = Chain(v, ve)
                ve.wait(v, p1_m8[NTC - 1])
                cur, oth = m8a, c2b
                step = 1
                while step < NT:
                    cv.tensor_copy(oth[0:8, 0:step], cur[0:8, 0:step])
                    cv.tensor_tensor(oth[0:8, step:NT], cur[0:8, step:NT],
                                     cur[0:8, 0:NT - step], ALU.add)
                    cur, oth = oth, cur
                    step *= 2
                cv.tensor_copy(oth[0:8, 0:1], cur[0:8, 0:1])
                cv.tensor_tensor(oth[0:8, 1:NT], cur[0:8, 1:NT],
                                 cur[0:8, 0:NT - 1], ALU.subtract)
                cv.scalar_tensor_tensor(cur[0:8, :], cur[0:8, :],
                                        eoff[:, 0:1], oth[0:8, :],
                                        ALU.add, ALU.mult)
                cv.tensor_scalar(cur[0:8, :], cur[0:8, :], capc[:, 0:1],
                                 None, ALU.min)
                p1b_dst[0] = ve.n
                p1b_cur[0] = cur

            for nb in range(NT // 512):
                @block.tensor
                def _(t, nb=nb):
                    ve.wait(t, p1b_dst[0] if nb == 0 else p1b_ve[nb - 1])
                    pe.inc(t.matmul(dstps[0:1, :], ones8f[:, 0:1],
                                    p1b_cur[0][0:8, nb * 512:(nb + 1) * 512],
                                    start=True, stop=True))

                @block.vector
                def _(v, nb=nb):
                    cv = Chain(v, ve)
                    pe.wait(v)
                    cv.tensor_scalar(dsr[0:1, :], dstps[0:1, :],
                                     float(NCAP - 1), None, ALU.min)

                for k in range(4):
                    @block.tensor
                    def _(t, nb=nb, k=k):
                        ve.wait(t)
                        pe.inc(t.transpose(dcps[:, :],
                                           dsr[0:1, k * 128:(k + 1) * 128],
                                           ident_f[0:1, 0:1]))

                    @block.vector
                    def _(v, nb=nb, k=k):
                        cv = Chain(v, ve)
                        pe.wait(v)
                        cv.tensor_copy(dsti[:, nb * 4 + k:nb * 4 + k + 1],
                                       dcps[:, :])
                        p1b_ve[nb] = ve.n

            for tc in range(NTC):
                @block.sync
                def _(sy, tc=tc):
                    if tc < 2:
                        s_xsw[0].wait(sy)
                        s_xsw[1].wait(sy)
                        pe.wait(sy)
                        sy.wait_ge(s_xt.h, 16 * NTC)
                    else:
                        g_sc[tc % 2].wait(sy)
                    sy.dma_start(stg[tc % 2][:, :],
                                 xt_d[tc * 128:(tc + 1) * 128, :]
                                 ).then_inc(s_xin[tc % 2].h, 16)
                    s_xin[tc % 2].n += 16

                @block.gpsimd
                def _(g, tc=tc):
                    s_xin[tc % 2].wait(g)
                    if tc == 0:
                        a_z0.wait(g)
                        g.wait_ge(ve.h, p1b_ve[NT // 512 - 1])
                    g.indirect_dma_start(
                        xg_d[:, :],
                        bass.IndirectOffsetOnAxis(dsti[:, tc:tc + 1], 0),
                        stg[tc % 2][:, :], None,
                    ).then_inc(g_sc[tc % 2].h, 16)
                    g_sc[tc % 2].n += 16


        p1es.close()
        p2es = ExitStack()

        def sb2(name, shape, dt):
            return p2es.enter_context(nc.sbuf_tensor(name, shape, dt))

        xtm = [sb2("xtm0", [128, E], BF16), sb2("xtm1", [128, E], BF16)]
        xgfm = sb2("xgfm", [128, EC * CAP], BF16)
        h1p = sb2("h1p", [128, HC * CAP], BF16)
        assert EC * CAP >= 2 * E
        zsc = [xgfm[:, 0:E], xgfm[:, E:2 * E]]

        p2_peA = {}
        p2_peB = {}
        p2_veh1 = {}
        p2_trcc = {}
        p2_trv = {}
        AB = [(0, min(512, CAP))] + ([(512, CAP)] if CAP > 512 else [])

        with nc.psum_tensor("trp0", [128, 128], BF16) as trp0, \
             nc.psum_tensor("trp1", [128, 128], BF16) as trp1, \
             nc.psum_tensor("h1ps0", [128, CAP], F32) as h1ps0, \
             nc.psum_tensor("h1ps1", [128, CAP], F32) as h1ps1, \
             nc.psum_tensor("zpsr", [128, E], F32) as zpsr:
            trp = [trp0, trp1]
            h1ps = [h1ps0, h1ps1]

            def emit_load(gcc):
                e2, cc2 = divmod(gcc, CAPC)
                if e2 >= NE:
                    return
                @block.sync
                def _(sy):
                    if gcc < 2:
                        g_sc[0].wait(sy)
                        g_sc[1].wait(sy)
                    else:
                        sy.wait_ge(pe.h, p2_trcc[gcc - 2])
                    sy.dma_start(
                        xtm[gcc % 2][:, :],
                        xg_d[e2 * CAP + cc2 * 128: e2 * CAP + (cc2 + 1) * 128, :],
                    ).then_inc(s_xg[gcc % 2].h, 16)
                    s_xg[gcc % 2].n += 16

            emit_load(0)
            emit_load(1)

            for e in range(NE):
                for cc in range(CAPC):
                    gcc = e * CAPC + cc

                    @block.tensor
                    def _(t, e=e, cc=cc, gcc=gcc):
                        s_xg[gcc % 2].wait(t)
                        if gcc == 0:
                            ve.wait(t)
                        vb = ve.n
                        for c in range(EC):
                            w = vb if c < 2 else vb + c - 1
                            if w >= 1:
                                t.wait_ge(ve.h, w)
                            pe.inc(t.transpose(trp[c % 2][:, :],
                                               xtm[gcc % 2][:, c * 128:(c + 1) * 128],
                                               ident_bf[:, :]))
                        p2_trcc[gcc] = pe.n

                    @block.vector
                    def _(v, e=e, cc=cc, gcc=gcc):
                        cv = Chain(v, ve)
                        pb = pe.n - EC
                        if cc == 0 and e > 0:
                            v.wait_ge(pe.h, p2_peA[e - 1])
                            s_zg[0].wait(v)
                            s_zg[1].wait(v)
                        for c in range(EC):
                            v.wait_ge(pe.h, pb + c + 1)
                            cv.tensor_copy(
                                xgfm[:, c * CAP + cc * 128: c * CAP + (cc + 1) * 128],
                                trp[c % 2][:, :])
                        p2_trv[e] = ve.n

                    emit_load(gcc + 2)

                @block.tensor
                def _(t, e=e):
                    ve.wait(t, p2_trv[e])
                    a_w1.wait(t, 16 * EC * (e + 1))
                    vb = ve.n
                    for h in range(HC):
                        w = vb + h - 1
                        if w >= 1:
                            t.wait_ge(ve.h, w)
                        for d in range(EC):
                            for (n0, n1) in AB:
                                mm = t.matmul(
                                    h1ps[h % 2][:, n0:n1],
                                    w1sb[:, d * HID + h * 128: d * HID + (h + 1) * 128],
                                    xgfm[:, d * CAP + n0: d * CAP + n1],
                                    start=(d == 0), stop=(d == EC - 1))
                        pe.inc(mm)
                    p2_peA[e] = pe.n

                @block.vector
                def _(v, e=e):
                    cv = Chain(v, ve)
                    pb = p2_peA[e] - HC
                    if e > 0:
                        v.wait_ge(pe.h, p2_peB[(e - 1, CAPC - 1)])
                    for h in range(HC):
                        v.wait_ge(pe.h, pb + h + 1)
                        cv.tensor_scalar(
                            h1p[:, h * CAP:(h + 1) * CAP],
                            h1ps[h % 2][:, 0:CAP],
                            bneg_fm[:, e * HC + h:e * HC + h + 1],
                            nrb_fm[:, e * HC + h:e * HC + h + 1],
                            ALU.add, ALU.max)
                    p2_veh1[e] = ve.n

                zvb = [0]

                @block.tensor
                def _(t, e=e):
                    ve.wait(t, p2_veh1[e])
                    a_w2.wait(t, 16 * (HC // W2G) * (e + 1))
                    zvb[0] = ve.n
                    for tc in range(CAPC):
                        if tc >= 1:
                            t.wait_ge(ve.h, zvb[0] + tc)
                        for h in range(HC):
                            for half in range(EH):
                                mm = t.matmul(
                                    zpsr[:, half * EW:(half + 1) * EW],
                                    h1p[:, h * CAP + tc * 128: h * CAP + tc * 128 + 128],
                                    w2sb[:, h * E + half * EW: h * E + (half + 1) * EW],
                                    start=(h == 0), stop=(h == HC - 1))
                        pe.inc(mm)
                        p2_peB[(e, tc)] = pe.n

                z_cp = {}

                @block.vector
                def _(v, e=e):
                    cv = Chain(v, ve)
                    for tc in range(CAPC):
                        gtc = e * CAPC + tc
                        v.wait_ge(pe.h, p2_peB[(e, tc)])
                        if gtc >= 2:
                            v.wait_ge(s_zg[gtc % 2].h, 16 * (gtc // 2))
                        cv.tensor_copy(zsc[tc % 2], zpsr[:, :])
                        z_cp[tc] = ve.n

                @block.sync
                def _(sy, e=e):
                    for tc in range(CAPC):
                        gtc = e * CAPC + tc
                        sy.wait_ge(ve.h, z_cp[tc])
                        sy.dma_start(
                            zg_d[e * CAP + tc * 128: e * CAP + (tc + 1) * 128, :],
                            zsc[tc % 2]).then_inc(s_zg[gtc % 2].h, 16)
                        s_zg[gtc % 2].n += 16

                @block.scalar
                def _(s, e=e):
                    if e + 1 < NE:
                        s.wait_ge(g_cw.h, 32 * (e + 2))
                        s.wait_ge(pe.h, p2_peA[e])
                        for d in range(EC):
                            s.dma_start(
                                w1sb[:, d * HID:(d + 1) * HID],
                                w1b_d[(e + 1) * E + d * 128:
                                      (e + 1) * E + (d + 1) * 128, :]
                            ).then_inc(a_w1.h, 16)
                        a_w1.n += 16 * EC
                        s.wait_ge(pe.h, p2_peB[(e, CAPC - 1)])
                        for h0 in range(0, HC, W2G):
                            s.dma_start(
                                bass.AP(w2sb, h0 * E,
                                        [[HC * E, 128], [E, W2G], [1, E]]),
                                bass.AP(w2b_d, ((e + 1) * HID + h0 * 128) * E,
                                        [[E, 128], [128 * E, W2G], [1, E]]),
                            ).then_inc(a_w2.h, 16)
                        a_w2.n += 16 * (HC // W2G)

        p2C = {}
        with nc.psum_tensor("cps", [1, E], F32) as cps:
            c_order = [NE - 1] + list(range(NE - 1))
            for idx, e in enumerate(c_order):
                buf = w2sb if idx % 2 == 0 else w1sb
                awbase = [0]

                @block.scalar
                def _(s, idx=idx, e=e, buf=buf):
                    if idx == 0:
                        return
                    if idx == 1:
                        s.wait_ge(pe.h, p2_peA[NE - 1])
                    else:
                        s.wait_ge(pe.h, p2C[idx - 2])
                    for h0 in range(0, HC, W2G):
                        s.dma_start(
                            bass.AP(buf, h0 * E,
                                    [[HC * E, 128], [E, W2G], [1, E]]),
                            bass.AP(w2b_d, (e * HID + h0 * 128) * E,
                                    [[E, 128], [128 * E, W2G], [1, E]]),
                        ).then_inc(a_c[idx % 2].h, 16)
                    a_c[idx % 2].n += 16 * (HC // W2G)

                @block.tensor
                def _(t, idx=idx, e=e, buf=buf):
                    if idx == 0:
                        t.wait_ge(pe.h, p2_peB[(NE - 1, CAPC - 1)])
                        ve.wait(t, p2_veh1[NE - 1])
                    else:
                        a_c[idx % 2].wait(t)
                    for h in range(HC):
                        for half in range(EH):
                            mm = t.matmul(
                                cps[0:1, half * EW:(half + 1) * EW],
                                rb1_bf[:, e * HC + h:e * HC + h + 1],
                                buf[:, h * E + half * EW: h * E + (half + 1) * EW],
                                start=(idx == 0 and h == 0), stop=False,
                                skip_group_check=True)
                    pe.inc(mm)
                    p2C[idx] = pe.n

            @block.tensor
            def _(t):
                g_pre.wait(t, 16)
                for half in range(EH):
                    mm = t.matmul(cps[0:1, half * EW:(half + 1) * EW],
                                  ones8[:, 0:1],
                                  b2_bf[:, half * EW:(half + 1) * EW],
                                  start=False, stop=True,
                                  skip_group_check=True)
                pe.inc(mm)

            @block.vector
            def _(v):
                cv = Chain(v, ve)
                pe.wait(v)
                cv.tensor_copy(csb[:, :], cps[0:1, :])

        with nc.psum_tensor("cbps", [128, E], F32) as cbps:
            @block.tensor
            def _(t):
                ve.wait(t)
                for half in range(EH):
                    mm = t.matmul(cbps[:, half * EW:(half + 1) * EW],
                                  ones1[:, :], csb[0:1, half * EW:(half + 1) * EW],
                                  start=True, stop=True)
                pe.inc(mm)

            @block.vector
            def _(v):
                cv = Chain(v, ve)
                pe.wait(v)
                cv.tensor_copy(cbc[:, :], cbps[:, :])

        if debug:
            for tc in range(NTC):
                @block.gpsimd
                def _(g, tc=tc):
                    if tc == 0:
                        s_zg[0].wait(g)
                        s_zg[1].wait(g)
                        ve.wait(g)
                    s_pre.wait(g)
                    g.indirect_dma_start(
                        zsc[0], None,
                        zg_d[:, :],
                        bass.IndirectOffsetOnAxis(dsti[:, tc:tc + 1], 0),
                    ).then_inc(g_pre.h, 16)
                    g_pre.n += 16

                @block.sync
                def _(sy, tc=tc):
                    g_pre.wait(sy)
                    sy.dma_start(dz_d[tc * 128:(tc + 1) * 128, :],
                                 zsc[0]).then_inc(s_pre.h, 16)
                    s_pre.n += 16

            @block.sync
            def _(sy):
                s_zg[0].wait(sy)
                s_zg[1].wait(sy)
                sy.dma_start(dxs_d[:, :], xs_d[:, :]).then_inc(s_pre.h, 16)
                ve.wait(sy)
                sy.dma_start(dm_d[:, :], m_all[:, :]).then_inc(s_pre.h, 16)
                sy.dma_start(dc_d[:, :], cbc[:, :]).then_inc(s_pre.h, 16)
                s_pre.n += 48

        p2es.close()
        wes.close()
        jbc = sb("jbc", [128, T], F32)
        k_fm = sb("k_fm", [128, EC * T], BF16)
        v_tm = sb("v_tm", [128, (T // 128) * E], BF16)
        q_fm = sb("q_fm", [128, EC * T], BF16)
        o_fm = sb("o_fm", [128, EC * T], BF16)
        wbuf = sb("wbuf", [128, EC * E], BF16)
        ab_sb = sb("ab_sb", [8, E], BF16)
        bb_sb = sb("bb_sb", [128, EC * R], BF16)
        u_sb = sb("u_sb", [8, T], BF16)

        p3_pe = {}
        p3_pe[(-1, 'end')] = pe.n
        p3_uve = {}
        p3_sv = {}
        prev_sv_last = [None]

        @block.gpsimd
        def _(g):
            cg = Chain(g, ge)
            g.wait_ge(pe.h, p3_pe[(-1, 'end')])
            s_zg[0].wait(g)
            s_zg[1].wait(g)
            cg.iota(jbc[:, :], pattern=[[1, T]], base=0, channel_multiplier=0,
                    allow_small_or_imprecise_dtypes=True)
        ge_jbc = ge.n

        @block.vector
        def _(v):
            s_zg[0].wait(v)
            s_zg[1].wait(v)

        @block.scalar
        def _(s):
            s_zg[0].wait(s)
            s_zg[1].wait(s)

        for sq in range(NSEQ):
            t0 = sq * T
            pAes = ExitStack()
            xn3 = pAes.enter_context(
                nc.sbuf_tensor("xn3_" + str(sq), [128, EC * T], BF16))

            @block.sync
            def _(sy, sq=sq, t0=t0, xn3=xn3):
                if sq > 0:
                    pe.wait(sy)
                    sy.wait_ge(ve.h, prev_sv_last[0])
                    s_out[0].wait(sy)
                    s_out[1].wait(sy)
                else:
                    s_xsw[0].wait(sy)
                    s_xsw[1].wait(sy)
                    sy.wait_ge(pe.h, p3_pe[(-1, 'end')])
                    s_zg[0].wait(sy)
                    s_zg[1].wait(sy)
                    if debug:
                        s_pre.wait(sy)
                        g_pre.wait(sy)
                for c in range(EC):
                    sy.dma_start(xn3[:, c * T:(c + 1) * T],
                                 xs_d[c * 128:(c + 1) * 128, t0:t0 + T]
                                 ).then_inc(s_xn3.h, 16)
                s_xn3.n += 16 * EC

            def emit_u(bd, sq, xn3, first):
                @block.gpsimd
                def _(g):
                    w = p3_pe.get((sq, 'wfree'),
                                  p3_pe.get((sq - 1, 'end'), 0))
                    if w:
                        g.wait_ge(pe.h, w)
                    g.dma_start(
                        bb_sb[:, :],
                        bass.AP(bd, 0, [[R, 128], [128 * R, EC], [1, R]]),
                    ).then_inc(g_bb.h, 16)
                    g_bb.n += 16

                with nc.psum_tensor("ups_u" + str(sq) + str(g_bb.n),
                                    [8, T], F32) as ups:
                    @block.tensor
                    def _(t):
                        g_bb.wait(t)
                        s_xn3.wait(t)
                        ae.wait(t)
                        if first:
                            ve.wait(t)
                        if sq > 0 and prev_sv_last[0] is not None:
                            t.wait_ge(ve.h, prev_sv_last[0])
                        w = p3_pe.get((sq, 'wfree'), 0)
                        if w:
                            t.wait_ge(ve.h, p3_uve[(sq, 'last')])
                        for c in range(EC):
                            for nb in range(NPB):
                                mm = t.matmul(
                                    ups[:, nb * PB:(nb + 1) * PB],
                                    bb_sb[:, c * R:(c + 1) * R],
                                    xn3[:, c * T + nb * PB:
                                        c * T + (nb + 1) * PB],
                                    start=(c == 0), stop=(c == EC - 1))
                        pe.inc(mm)

                    @block.vector
                    def _(v):
                        cv = Chain(v, ve)
                        pe.wait(v)
                        cv.tensor_scalar(u_sb[:, :], ups[:, :], LORA_S, None,
                                         ALU.mult)
                        p3_uve[(sq, 'last')] = ve.n

            def proj_fm(wT_d, aT_d, bT_d, dest, scale, sq, tag, xn3,
                        first=False):
                emit_u(bT_d, sq, xn3, first)

                @block.gpsimd
                def _(g):
                    w = p3_pe.get((sq, 'wfree'),
                                  p3_pe.get((sq - 1, 'end'), 0))
                    if w:
                        g.wait_ge(pe.h, w)
                    for c in range(EC):
                        g.dma_start(wbuf[:, c * E:(c + 1) * E],
                                    wT_d[c * 128:(c + 1) * 128, :]
                                    ).then_inc(g_wf.h, 16)
                    g_wf.n += 16 * EC
                    g.dma_start(ab_sb[:, :], aT_d[:, :]).then_inc(g_ab.h, 16)
                    g_ab.n += 16

                with nc.psum_tensor("psA_" + tag + str(sq), [128, 4 * 512],
                                    F32) as psA:
                    for ic in range(EC):
                        @block.tensor
                        def _(t, ic=ic):
                            g_wf.wait(t)
                            g_ab.wait(t)
                            ve.wait(t, p3_uve[(sq, 'last')])
                            ae.wait(t)
                            for ec in range(EC):
                                for nb in range(NPB):
                                    t.matmul(
                                        psA[:, (nb % 4) * 512:
                                            (nb % 4) * 512 + PB],
                                        wbuf[:, ec * E + ic * 128:
                                             ec * E + ic * 128 + 128],
                                        xn3[:, ec * T + nb * PB:
                                            ec * T + (nb + 1) * PB],
                                        start=(ec == 0), stop=False)
                            for nb in range(NPB):
                                mm = t.matmul(
                                    psA[:, (nb % 4) * 512:(nb % 4) * 512 + PB],
                                    ab_sb[0:8, ic * 128:(ic + 1) * 128],
                                    u_sb[0:8, nb * PB:(nb + 1) * PB],
                                    start=False, stop=True)
                            pe.inc(mm)

                        @block.scalar
                        def _(s, ic=ic):
                            ca = Chain(s, ae)
                            pe.wait(s)
                            for nb in range(NPB):
                                ca.activation(
                                    dest[:, ic * T + nb * PB:
                                         ic * T + (nb + 1) * PB],
                                    psA[:, (nb % 4) * 512:(nb % 4) * 512 + PB],
                                    AF.Copy, scale=scale)
                p3_pe[(sq, 'wfree')] = pe.n

            proj_fm(wkT_d, akT_d, bkT_d, k_fm, 1.0, sq, "k", xn3,
                    first=(sq == 0))

            emit_u(bvT_d, sq, xn3, False)

            @block.gpsimd
            def _(g, sq=sq):
                g.wait_ge(pe.h, p3_pe[(sq, 'wfree')])
                for c in range(EC):
                    g.dma_start(wbuf[:, c * E:(c + 1) * E],
                                wvT_d[c * 128:(c + 1) * 128, :]
                                ).then_inc(g_wf.h, 16)
                g_wf.n += 16 * EC
                g.dma_start(ab_sb[:, :], avT_d[:, :]).then_inc(g_ab.h, 16)
                g_ab.n += 16

            with nc.psum_tensor("psV" + str(sq), [128, E], F32) as psV:
                for tc in range(T // 128):
                    @block.tensor
                    def _(t, tc=tc, sq=sq):
                        g_wf.wait(t)
                        g_ab.wait(t)
                        ve.wait(t, p3_uve[(sq, 'last')])
                        ae.wait(t)
                        for ec in range(EC):
                            for half in range(EH):
                                t.matmul(
                                    psV[:, half * EW:(half + 1) * EW],
                                    xn3[:, ec * T + tc * 128:
                                        ec * T + tc * 128 + 128],
                                    wbuf[:, ec * E + half * EW:
                                         ec * E + (half + 1) * EW],
                                    start=(ec == 0), stop=False)
                        for half in range(EH):
                            mm = t.matmul(
                                psV[:, half * EW:(half + 1) * EW],
                                u_sb[0:8, tc * 128: tc * 128 + 128],
                                ab_sb[0:8, half * EW:(half + 1) * EW],
                                start=False, stop=True)
                        pe.inc(mm)

                    @block.scalar
                    def _(s, tc=tc):
                        ca = Chain(s, ae)
                        pe.wait(s)
                        ca.activation(v_tm[:, tc * E:(tc + 1) * E],
                                      psV[:, :], AF.Copy)
            p3_pe[(sq, 'wfree')] = pe.n

            proj_fm(wqT_d, aqT_d, bqT_d, q_fm, SCL, sq, "q", xn3)
            proj_ae_done = ae.n

            @block.gpsimd
            def _(g, sq=sq):
                g.wait_ge(pe.h, p3_pe[(sq, 'wfree')])
                for c in range(EC):
                    g.dma_start(wbuf[:, c * E:(c + 1) * E],
                                wpT_d[c * 128:(c + 1) * 128, :]
                                ).then_inc(g_wf.h, 16)
                g_wf.n += 16 * EC

            pAes.close()
            pB1 = ExitStack()

            def sb3(name, shape, dt):
                return pB1.enter_context(
                    nc.sbuf_tensor(name + "_" + str(sq), shape, dt))

            colfac = sb3("colfac", [128, T], BF16)
            praw = [sb3("praw0", [128, T], BF16), sb3("praw1", [128, T], BF16)]
            pnrm = [sb3("pnrm0", [128, T], BF16), sb3("pnrm1", [128, T], BF16)]
            pt_sb = sb3("pt_sb", [128, T], BF16)

            with nc.psum_tensor("sbig" + str(sq), [128, SW], F32) as sbig, \
                 nc.psum_tensor("ptp0" + str(sq), [128, 128], BF16) as ptp0, \
                 nc.psum_tensor("ptp1" + str(sq), [128, 128], BF16) as ptp1, \
                 nc.psum_tensor("ops0" + str(sq), [64, 128], F32) as ops0, \
                 nc.psum_tensor("ops1" + str(sq), [64, 128], F32) as ops1:
                ptp = [ptp0, ptp1]
                ops = [ops0, ops1]
                iters = []
                last_user = {0: None, 1: None}
                it = 0
                for h in range(H):
                    for qt in range(QTN):
                        q0 = qt * 128
                        W = (qt + 1) * 128
                        covS = min(((q0 + 128 + 511) // 512) * 512, T)
                        small = covS <= SW // 2
                        soff = (SW // 2) * (it % 2) if small else 0
                        regions = [it % 2] if small else [0, 1]
                        deps = [last_user[r] for r in regions
                                if last_user[r] is not None]
                        dep = max(deps) if deps else None
                        for r in regions:
                            last_user[r] = it
                        iters.append(dict(h=h, qt=qt, q0=q0, W=W, covS=covS,
                                          soff=soff, dep=dep))
                        it += 1
                NIT = len(iters)
                p3_exp = {}
                p3_oev = {}
                p3_pnm = {}
                p3_rmx = {}
                p3_tr = {}
                ae_base = ae.n
                cnt = ae_base
                for i in range(NIT):
                    if iters[i]["qt"] == 0:
                        cnt += 1
                    cnt += 1
                    p3_exp[i] = cnt
                ae_hloop_end = cnt

                def emit_S(i):
                    d = iters[i]

                    @block.tensor
                    def _(t, i=i, d=d, sq=sq, proj_ae_done=proj_ae_done):
                        if d["dep"] is not None:
                            t.wait_ge(ae.h, p3_exp[d["dep"]])
                        elif sq > 0 and prev_sv_last[0] is not None:
                            t.wait_ge(ve.h, prev_sv_last[0])
                        if i == 0:
                            t.wait_ge(ae.h, proj_ae_done)
                        hb = (d["h"] % 2) * 64
                        hk = d["h"] // 2
                        nkc = (d["covS"] + 511) // 512
                        for nk in range(nkc):
                            n0 = nk * 512
                            n1 = min(n0 + 512, d["covS"])
                            mm = t.matmul(
                                sbig[:, d["soff"] + n0: d["soff"] + n1],
                                q_fm[hb:hb + 64,
                                     hk * T + d["q0"]: hk * T + d["q0"] + 128],
                                k_fm[hb:hb + 64, hk * T + n0: hk * T + n1],
                                start=True, stop=True)
                        pe.inc(mm)

                def emit_maskmax(i):
                    d = iters[i]

                    @block.vector
                    def _(v, i=i, d=d):
                        cv = Chain(v, ve)
                        pe.wait(v)
                        if i >= 1:
                            v.wait_ge(ae.h, p3_exp[i - 1])
                        cv.tensor_tensor(
                            sbig[:, d["soff"] + d["q0"]: d["soff"] + d["q0"] + 128],
                            mtri[:, :],
                            sbig[:, d["soff"] + d["q0"]: d["soff"] + d["q0"] + 128],
                            ALU.add)
                        cv.reduce_max(lnc[:, 0:1],
                                      sbig[:, d["soff"]:d["soff"] + d["W"]],
                                      axis=AXL.X, negate=True)
                        p3_rmx[i] = ve.n

                emit_S(0)
                emit_maskmax(0)

                for i in range(NIT):
                    d = iters[i]
                    qt = d["qt"]

                    if i + 1 < NIT:
                        emit_S(i + 1)
                        emit_maskmax(i + 1)

                    @block.vector
                    def _(v, i=i, d=d):
                        cv = Chain(v, ve)
                        v.wait_ge(ae.h, p3_exp[i])
                        if i >= 2:
                            v.wait_ge(pe.h, p3_tr[i - 2])
                        cv.scalar_tensor_tensor(pnrm[i % 2][:, 0:d["W"]],
                                                praw[i % 2][:, 0:d["W"]], 1.0,
                                                colfac[:, 0:d["W"]],
                                                ALU.mult, ALU.mult,
                                                accum_out=lnc[:, 1:2])
                        cv.reciprocal(lnc[:, 2:3], lnc[:, 1:2])
                        cv.tensor_scalar(pnrm[i % 2][:, 0:d["W"]],
                                         pnrm[i % 2][:, 0:d["W"]],
                                         lnc[:, 2:3], None, ALU.mult)
                        p3_pnm[i] = ve.n

                    @block.tensor
                    def _(t, i=i, qt=qt):
                        ve.wait(t, p3_pnm[i])
                        vb = ve.n
                        for kc in range(qt + 1):
                            w = vb + kc - 1
                            if w > p3_pnm[i]:
                                t.wait_ge(ve.h, w)
                            pe.inc(t.transpose(
                                ptp[kc % 2][:, :],
                                pnrm[i % 2][:, kc * 128:(kc + 1) * 128],
                                ident_bf[:, :]))
                        p3_tr[i] = pe.n

                    @block.vector
                    def _(v, i=i, qt=qt):
                        cv = Chain(v, ve)
                        pb = pe.n - (qt + 1)
                        for kc in range(qt + 1):
                            v.wait_ge(pe.h, pb + kc + 1)
                            cv.tensor_copy(pt_sb[:, kc * 128:(kc + 1) * 128],
                                           ptp[kc % 2][:, :])

                    @block.tensor
                    def _(t, i=i, qt=qt, d=d):
                        vb = ve.n - (qt + 1)
                        if i >= 2:
                            t.wait_ge(ve.h, p3_oev[i - 2])
                        for kc in range(qt + 1):
                            t.wait_ge(ve.h, vb + kc + 1)
                            mm = t.matmul(
                                ops[i % 2][:, :],
                                v_tm[:, kc * E + d["h"] * DH:
                                     kc * E + (d["h"] + 1) * DH],
                                pt_sb[:, kc * 128:(kc + 1) * 128],
                                start=(kc == 0), stop=(kc == qt))
                        pe.inc(mm)

                    @block.vector
                    def _(v, i=i, qt=qt, d=d):
                        cv = Chain(v, ve)
                        pe.wait(v)
                        hb = (d["h"] % 2) * 64
                        hk = d["h"] // 2
                        cv.tensor_copy(
                            o_fm[hb:hb + 64,
                                 hk * T + qt * 128: hk * T + qt * 128 + 128],
                            ops[i % 2][:, :])
                        p3_oev[i] = ve.n

                @block.scalar
                def _(s):
                    ca = Chain(s, ae)
                    for i in range(NIT):
                        d = iters[i]
                        if d["qt"] == 0:
                            h = d["h"]
                            ge.wait(s, ge_jbc)
                            if i >= 1:
                                s.wait_ge(ve.h, p3_pnm[i - 1])
                            ca.activation(colfac[:, :], jbc[:, :], AF.Exp,
                                          scale=-slopes[h])
                        s.wait_ge(ve.h, p3_rmx[i])
                        if i >= 2:
                            s.wait_ge(ve.h, p3_pnm[i - 2])
                        ca.activation(praw[i % 2][:, 0:d["W"]],
                                      sbig[:, d["soff"]:d["soff"] + d["W"]],
                                      AF.Exp, bias=lnc[:, 0:1])
                    assert ae.n == ae_hloop_end, (ae.n, ae_hloop_end)

                pB1.close()
                pB2 = ExitStack()

                def sb4(name, shape, dt):
                    return pB2.enter_context(
                        nc.sbuf_tensor(name + "_" + str(sq), shape, dt))

                ztb = [sb4("ztb0", [128, E], BF16), sb4("ztb1", [128, E], BF16)]
                xtb = [sb4("xtb0", [128, E], F32), sb4("xtb1", [128, E], F32)]
                gts = sb4("gts", [128, E], BF16)
                otb = [sb4("otb0", [128, E], F32), sb4("otb1", [128, E], F32)]

                for gi0 in range(min(2, QTN)):
                    @block.gpsimd
                    def _(g, gi0=gi0, sq=sq, t0=t0):
                        s_zg[0].wait(g)
                        s_zg[1].wait(g)
                        pe.wait(g)
                        ae.wait(g)
                        ve.wait(g)
                        g.indirect_dma_start(
                            ztb[gi0 % 2][:, :], None,
                            zg_d[:, :],
                            bass.IndirectOffsetOnAxis(
                                dsti[:, t0 // 128 + gi0: t0 // 128 + gi0 + 1], 0),
                        ).then_inc(g_zi[gi0 % 2].h, 16)
                        g_zi[gi0 % 2].n += 16

                    @block.sync
                    def _(sy, gi0=gi0, sq=sq, t0=t0):
                        pe.wait(sy)
                        sy.dma_start(
                            xtb[gi0 % 2][:, :],
                            x_d[t0 + gi0 * 128: t0 + (gi0 + 1) * 128, :]
                        ).then_inc(s_xe[gi0 % 2].h, 16)
                        s_xe[gi0 % 2].n += 16

                for qt in range(QTN):
                    @block.tensor
                    def _(t, qt=qt, sq=sq):
                        g_wf.wait(t)
                        if qt == 0:
                            ve.wait(t)
                        if qt >= 2:
                            ve.wait(t, p3_sv[(sq, qt - 2)])
                        for dc in range(EC):
                            for half in range(EH):
                                mm = t.matmul(
                                    sbig[:, (qt % 2) * E + half * EW:
                                         (qt % 2) * E + (half + 1) * EW],
                                    o_fm[:, dc * T + qt * 128:
                                         dc * T + qt * 128 + 128],
                                    wbuf[:, dc * E + half * EW:
                                         dc * E + (half + 1) * EW],
                                    start=(dc == 0), stop=(dc == EC - 1))
                        pe.inc(mm)

                    @block.vector
                    def _(v, qt=qt, sq=sq):
                        cv = Chain(v, ve)
                        bi = qt % 2
                        pe.wait(v)
                        g_zi[bi].wait(v)
                        gidx = sq * QTN + qt
                        if gidx >= 2:
                            v.wait_ge(s_out[gidx % 2].h, 16 * (gidx // 2))
                        cv.tensor_tensor(otb[bi][:, :], ztb[bi][:, :],
                                         cbc[:, :], ALU.add)

                    @block.scalar
                    def _(s, qt=qt):
                        ca = Chain(s, ae)
                        ve.wait(s)
                        ca.activation(gts[:, :], otb[qt % 2][:, :], AF.Silu)

                    @block.vector
                    def _(v, qt=qt, sq=sq):
                        cv = Chain(v, ve)
                        bi = qt % 2
                        ae.wait(v)
                        s_xe[bi].wait(v)
                        cv.tensor_tensor(otb[bi][:, :],
                                         sbig[:, bi * E:(bi + 1) * E],
                                         gts[:, :], ALU.mult)
                        cv.tensor_tensor(otb[bi][:, :], otb[bi][:, :],
                                         xtb[bi][:, :], ALU.add)
                        p3_sv[(sq, qt)] = ve.n

                    @block.sync
                    def _(sy, qt=qt, sq=sq, t0=t0):
                        bi = qt % 2
                        sy.wait_ge(ve.h, p3_sv[(sq, qt)])
                        sy.dma_start(
                            out_d[t0 + qt * 128: t0 + (qt + 1) * 128, :],
                            otb[bi][:, :]).then_inc(s_out[(sq * QTN + qt) % 2].h, 16)
                        s_out[(sq * QTN + qt) % 2].n += 16
                        gi2 = qt + 2
                        if gi2 < QTN:
                            sy.dma_start(
                                xtb[gi2 % 2][:, :],
                                x_d[t0 + gi2 * 128: t0 + (gi2 + 1) * 128, :]
                            ).then_inc(s_xe[gi2 % 2].h, 16)
                            s_xe[gi2 % 2].n += 16

                    @block.gpsimd
                    def _(g, qt=qt, sq=sq, t0=t0):
                        gi2 = qt + 2
                        if gi2 < QTN:
                            g.wait_ge(ve.h, p3_sv[(sq, qt)])
                            g.indirect_dma_start(
                                ztb[gi2 % 2][:, :], None,
                                zg_d[:, :],
                                bass.IndirectOffsetOnAxis(
                                    dsti[:, t0 // 128 + gi2: t0 // 128 + gi2 + 1], 0),
                            ).then_inc(g_zi[gi2 % 2].h, 16)
                            g_zi[gi2 % 2].n += 16

                prev_sv_last[0] = p3_sv[(sq, QTN - 1)]
            p3_pe[(sq, 'end')] = pe.n
            pB2.close()

            if debug and sq == NSEQ - 1:
                @block.gpsimd
                def _(g):
                    ae.wait(g)
                    ve.wait(g)
                    for (dd, ss) in ((dq_d, q_fm), (dk_d, k_fm), (dv_d, v_tm),
                                     (do_d, o_fm)):
                        g.dma_start(dd[:, :], ss[:, :]).then_inc(g_pre.h, 16)
                        g_pre.n += 16

        @block.sync
        def _(sy):
            sy.wait_ge(s_out[0].h, s_out[0].n)
            sy.wait_ge(s_out[1].h, s_out[1].n)
            if debug:
                sy.wait_ge(s_pre.h, s_pre.n)
                sy.wait_ge(g_pre.h, g_pre.n)

    return nc


_NC_CACHE = {}


def _get_nc(key, cfg, debug=False):
    if key not in _NC_CACHE:
        _NC_CACHE[key] = build(cfg, debug=debug)
    return _NC_CACHE[key]


def make_in_maps(inputs, cfg, ncore=8):
    E, T, NSEQ, HID = cfg["E"], cfg["T"], cfg["NSEQ"], cfg["HID"]
    x = np.asarray(inputs["L0_attn_x"], np.float32)
    B = x.shape[0]
    per = B // ncore
    assert per * ncore == B and per == NSEQ

    def t(a):
        return np.ascontiguousarray(np.asarray(a, np.float32).T)

    com = {
        "wqT": t(inputs["L0_attn_Wq"]), "wkT": t(inputs["L0_attn_Wk"]),
        "wvT": t(inputs["L0_attn_Wv"]), "wpT": t(inputs["L0_attn_Wp"]),
        "bqT": t(inputs["L0_attn_Bq"]), "bkT": t(inputs["L0_attn_Bk"]),
        "bvT": t(inputs["L0_attn_Bv"]),
        "aqT": t(inputs["L0_attn_Aq"]), "akT": t(inputs["L0_attn_Ak"]),
        "avT": t(inputs["L0_attn_Av"]),
        "w1": np.ascontiguousarray(
            np.asarray(inputs["L0_ffn_W1"], np.float32).reshape(-1, HID)),
        "b1": np.asarray(inputs["L0_ffn_b1"], np.float32),
        "w2": np.ascontiguousarray(
            np.asarray(inputs["L0_ffn_W2"], np.float32).reshape(-1, E)),
        "b2": np.asarray(inputs["L0_ffn_b2"], np.float32),
        "wr": np.asarray(inputs["L0_ffn_Wr"], np.float32),
        "br": np.ascontiguousarray(
            np.asarray(inputs["L0_ffn_br"], np.float32).reshape(-1, 1)),
    }
    in_maps = []
    for c in range(ncore):
        m = dict(com)
        m["x"] = np.ascontiguousarray(
            x[c * per:(c + 1) * per].reshape(per * T, E))
        in_maps.append(m)
    return in_maps


def kernel(**inputs):
    cfg = CFG_FULL
    NCORE = 8
    T, E = cfg["T"], cfg["E"]
    in_maps = make_in_maps(inputs, cfg, NCORE)
    per = cfg["NSEQ"]
    nc = _get_nc("full", cfg)
    res = run_bass_kernel_spmd(nc, in_maps, core_ids=list(range(NCORE)))
    out = np.concatenate(
        [np.asarray(r["out"]).reshape(per, T, E) for r in res.results], axis=0)
    return out.astype(np.float32)
